# version 03
# baseline (speedup 1.0000x reference)
import numpy as np
import ml_dtypes
import concourse.bass as bass
import concourse.mybir as mybir
from concourse.bass_utils import run_bass_kernel_spmd
from contextlib import ExitStack

F32 = mybir.dt.float32
BF16 = mybir.dt.bfloat16
AF = mybir.ActivationFunctionType
ALU = mybir.AluOpType
AX = mybir.AxisListType

D = 1024
SEQ = 4096
NQ = 2048
HFF = 4096
NHEAD = 16
NPAIR = 8
MAGIC = 12582912.0
EPS = 1e-5

SEMS = ("LD", "GD", "SD", "T", "V", "S", "ST")


class Rec:
    def __init__(self):
        self.q = {k: [] for k in ("sync", "gpsimd", "tensor", "vector", "scalar")}
        self.c = {k: 0 for k in SEMS}

    def op(self, eng, fn, inc=None, k=1):
        self.q[eng].append(("op", fn, inc, k))
        if inc is not None:
            self.c[inc] += k
            return (inc, self.c[inc])
        return None

    def dma(self, eng, fn, inc="LD"):
        return self.op(eng, fn, inc, 16)

    def wait(self, eng, tok):
        if tok is not None:
            self.q[eng].append(("wait", tok[0], tok[1]))

    def now(self, sem):
        return (sem, self.c[sem])


def _replay(engine, items, sems):
    for it in items:
        if it[0] == "wait":
            engine.wait_ge(sems[it[1]], it[2])
        else:
            ins = it[1](engine)
            if it[2] is not None:
                ins.then_inc(sems[it[2]], it[3])


AR_N = 51392
SM_N = 896


def build(taps=()):
    taps = set(taps)
    nc = bass.Bass("TRN2", target_bir_lowering=False, debug=False)

    def din(name, shape, dt=F32):
        return nc.dram_tensor(name, shape, dt, kind="ExternalInput").ap()

    x_t = din("x_t", [D, SEQ])
    wqT = din("wqT", [D, D]); wkT = din("wkT", [D, D])
    wvT = din("wvT", [D, D]); woT = din("woT", [D, D])
    bqc = din("bqc", [128, 8]); bkc = din("bkc", [128, 8])
    bvc = din("bvc", [128, 8]); boc = din("boc", [128, 8])
    lnwc = din("lnwc", [128, 8]); lnbc = din("lnbc", [128, 8])
    b1c = din("b1c", [128, 32]); b2c = din("b2c", [128, 8])
    w1T = din("w1T", [D, HFF]); w2T = din("w2T", [HFF, D])
    ropeCK = din("ropeCK", [128, SEQ], BF16); ropeSK = din("ropeSK", [128, SEQ], BF16)
    ropeCQ = din("ropeCQ", [128, NQ], BF16); ropeSQ = din("ropeSQ", [128, NQ], BF16)

    out_d = nc.dram_tensor("out", [D, NQ], F32, kind="ExternalOutput").ap()

    def dscr(name, shape, dt):
        return nc.dram_tensor(name, shape, dt, kind="Internal").ap()

    scr_rows = dscr("scr_rows", [16, SEQ], F32)
    scr_c = dscr("scr_c", [1, 256], F32)
    scr_K = dscr("scr_K", [NPAIR, 128, SEQ], BF16)
    scr_Q = dscr("scr_Q", [NPAIR, 128, NQ], BF16)
    scr_V = dscr("scr_V", [16, 32, 128, 65], BF16)
    scr_W1q = dscr("scr_W1q", [8, 128, HFF], BF16)
    scr_W2q = dscr("scr_W2q", [32, 128, D], BF16)

    TAPDEFS = {
        "stats": ([128, 256], F32), "xn2": ([128, SEQ], F32), "xq": ([128, NQ], F32),
        "cneg": ([1, 256], F32), "att": ([128, NQ], F32), "g0": ([128, NQ], F32),
        "w1q": ([128, HFF], F32), "h0": ([128, 256], F32), "pvs": ([64, NQ], F32),
        "wsc": ([128, 64], F32),
        "lr": ([1, NQ], F32),
    }
    tap_d = {t: nc.dram_tensor("tap_" + t, TAPDEFS[t][0], TAPDEFS[t][1],
                               kind="ExternalOutput").ap() for t in taps}

    r = Rec()

    with ExitStack() as ctx:
        e = ctx.enter_context
        AR = e(nc.sbuf_tensor("AR", [128, AR_N], F32))
        SM = e(nc.sbuf_tensor("SM", [128, SM_N], F32))
        AR_B = AR.ap().bitcast(BF16)
        SM_B = SM.ap().bitcast(BF16)

        def fv(off, n, p0=0, p1=128):
            return AR[p0:p1, off:off + n]

        def bvv(off, nb, p0=0, p1=128):
            return AR_B[p0:p1, 2 * off:2 * off + nb]

        def sv(off, n, p0=0, p1=128):
            return SM[p0:p1, off:off + n]

        ones_bf = SM_B[:, 0:1]
        ones_f = sv(1, 1)
        SCOL = 8
        def scol(i, n=32):
            return sv(SCOL + 32 * i, n)
        W1S = sv(300, 8); W2S = sv(308, 32)
        WSC = sv(340, 4)
        BQC = sv(360, 8); BKC = sv(368, 8); BVC = sv(376, 8); BOC = sv(384, 8)
        LNW = sv(392, 8); LNB = sv(400, 8)
        B1C = sv(408, 32); B2C = sv(440, 8)
        CBIAS = sv(448, 64)
        NEGC = sv(512, 32, 0, 32)
        NEGCT = sv(544, 32, 0, 32)
        CFIN = sv(576, 1, 0, 32)
        WS11 = sv(578, 1, 0, 1); WS12 = sv(579, 1, 0, 1)
        WS21 = sv(580, 1, 0, 1); WS22 = sv(581, 1, 0, 1)
        W1T_ = sv(584, 1); W2T_ = sv(585, 1)
        EPSC = sv(586, 1)
        TCOL = sv(608, 128)
        XS2 = sv(736, 64)
        XS3 = sv(800, 64)
        CF2 = sv(592, 1, 0, 32)

        PS_BIG = e(nc.psum_tensor("PS_BIG", [128, 2048], F32))
        PV0 = e(nc.psum_tensor("PV0", [128, 512], F32))
        PV1 = e(nc.psum_tensor("PV1", [128, 512], F32))
        PSX = e(nc.psum_tensor("PSX", [128, 512], F32))
        PSY = e(nc.psum_tensor("PSY", [128, 512], F32))

        sems = {k: e(nc.semaphore(k)) for k in SEMS}
        block = e(nc.Block())

        def tap_barrier(tok):
            for _eng in ("sync", "vector", "scalar", "tensor", "gpsimd"):
                r.wait(_eng, tok)

        VOPT = 1

        def vop(fn):
            if VOPT == 0:
                return r.op("vector", fn, "V")
            if VOPT == 2:
                tok = r.op("vector", fn, "V")
                r.op("vector", lambda v: v.engine_nop())
                r.op("vector", lambda v: v.engine_nop())
                return tok
            tok = r.op("vector", fn, "V")
            r.wait("vector", tok)
            return tok

        def sop(fn):
            if VOPT == 0:
                return r.op("scalar", fn, "S")
            if VOPT == 2:
                tok = r.op("scalar", fn, "S")
                r.op("scalar", lambda x: x.engine_nop())
                r.op("scalar", lambda x: x.engine_nop())
                return tok
            tok = r.op("scalar", fn, "S")
            r.wait("scalar", tok)
            return tok


        xchunk = [fv(0, 4096), fv(4096, 4096)]
        xbuf = bvv(8192, 4096)
        sqbuf = bvv(10240, 4096)
        wchunk = [fv(12288, 4096), fv(16384, 4096)]

        r.op("vector", lambda v: v.memset(ones_bf, 1.0))
        r.op("vector", lambda v: v.memset(ones_f, 1.0))
        r.op("vector", lambda v: v.memset(EPSC, EPS), "V")
        for dst, src in ((BQC, bqc), (BKC, bkc), (BVC, bvc), (BOC, boc),
                         (LNW, lnwc), (LNB, lnbc), (B1C, b1c), (B2C, b2c)):
            r.dma("gpsimd", lambda g, dst=dst, src=src: g.dma_start(out=dst, in_=src), "GD")

        cast_toks = [None] * 8
        stats_mm_tok = None
        r.wait("tensor", r.now("V"))
        r.op("tensor", lambda t: t.matmul(out=PSX[:, 500:501], lhsT=SM_B[:, 0:128], rhs=ones_bf, start=True, stop=True))
        r.op("tensor", lambda t: t.matmul(out=PSY[:, 500:501], lhsT=SM_B[:, 0:128], rhs=ones_bf, start=True, stop=True))
        wred_toks = [None] * 40
        for dc in range(8):
            b = dc % 2
            if dc >= 2:
                r.wait("sync", cast_toks[dc - 2])
            xd = r.dma("sync", lambda s, dc=dc, b=b: s.dma_start(
                out=xchunk[b], in_=x_t[dc * 128:(dc + 1) * 128, :]))
            r.wait("vector", xd)
            if stats_mm_tok is not None:
                r.wait("vector", stats_mm_tok)
            r.op("vector", lambda v, b=b: v.tensor_copy(out=xbuf, in_=xchunk[b]))
            cast_toks[dc] = r.op("vector", lambda v: v.tensor_tensor(
                out=sqbuf, in0=xbuf, in1=xbuf, op=ALU.mult), "V")
            r.wait("tensor", cast_toks[dc])
            for tt in range(32):
                r.op("tensor", lambda t, tt=tt, dc=dc: t.matmul(
                    out=PSX[:, tt:tt + 1], lhsT=xbuf[:, tt * 128:(tt + 1) * 128],
                    rhs=ones_bf, start=False, stop=True))
            for tt in range(32):
                stats_mm_tok = r.op("tensor", lambda t, tt=tt, dc=dc: t.matmul(
                    out=PSY[:, tt:tt + 1], lhsT=sqbuf[:, tt * 128:(tt + 1) * 128],
                    rhs=ones_bf, start=False, stop=True),
                    "T" if tt == 31 else None)
        stats_done = stats_mm_tok

        for i in range(40):
            b = i % 2
            if i >= 2:
                r.wait("gpsimd", wred_toks[i - 2])
            if i < 8:
                wd = r.dma("gpsimd", lambda g, i=i, b=b: g.dma_start(
                    out=wchunk[b], in_=w1T[i * 128:(i + 1) * 128, :]), "GD")
                fn = lambda v, i=i, b=b: v.tensor_reduce(
                    out=W1S[:, i:i + 1], in_=wchunk[b], axis=AX.X, op=ALU.add,
                    apply_absolute_value=True)
            else:
                wd = r.dma("gpsimd", lambda g, i=i, b=b: g.dma_start(
                    out=wchunk[b][:, 0:1024], in_=w2T[(i - 8) * 128:(i - 7) * 128, :]), "GD")
                fn = lambda v, i=i, b=b: v.tensor_reduce(
                    out=W2S[:, i - 8:i - 7], in_=wchunk[b][:, 0:1024], axis=AX.X,
                    op=ALU.add, apply_absolute_value=True)
            r.wait("vector", wd)
            wred_toks[i] = r.op("vector", fn, "V")
        wred_done = wred_toks[39]

        r.op("vector", lambda v: v.tensor_reduce(out=W1T_, in_=W1S, axis=AX.X, op=ALU.add))
        tot_tok = r.op("vector", lambda v: v.tensor_reduce(out=W2T_, in_=W2S, axis=AX.X, op=ALU.add), "V")
        r.wait("tensor", tot_tok)
        r.op("tensor", lambda t: t.matmul(out=PSX[0:1, 40:41], lhsT=W1T_, rhs=ones_f, start=False, stop=True))
        wtot_mm = r.op("tensor", lambda t: t.matmul(out=PSX[0:1, 41:42], lhsT=W2T_, rhs=ones_f, start=False, stop=True), "T")
        r.wait("vector", wtot_mm)
        vop(lambda v: v.tensor_scalar(out=WS11, in0=PSX[0:1, 40:41], scalar1=1.0 / (D * HFF), scalar2=EPS, op0=ALU.mult, op1=ALU.max))
        vop(lambda v: v.tensor_scalar(out=WS21, in0=PSX[0:1, 41:42], scalar1=1.0 / (D * HFF), scalar2=EPS, op0=ALU.mult, op1=ALU.max))
        vop(lambda v: v.reciprocal(out=WS12, in_=WS11))
        ws_tok = vop(lambda v: v.reciprocal(out=WS22, in_=WS21))
        r.wait("gpsimd", ws_tok)
        r.dma("gpsimd", lambda g: g.dma_start(out=scr_rows[6:7, 0:1], in_=WS11), "GD")
        r.dma("gpsimd", lambda g: g.dma_start(out=scr_rows[6:7, 1:2], in_=WS12), "GD")
        r.dma("gpsimd", lambda g: g.dma_start(out=scr_rows[6:7, 2:3], in_=WS21), "GD")
        t4 = r.dma("gpsimd", lambda g: g.dma_start(out=scr_rows[6:7, 3:4], in_=WS22), "GD")
        r.wait("gpsimd", t4)
        wsbc_tok = r.dma("gpsimd", lambda g: g.dma_start(
            out=WSC, in_=scr_rows[6:7, 0:4].partition_broadcast(128)), "GD")
        if "wsc" in taps:
            r.wait("gpsimd", wsbc_tok)
            tb = r.dma("gpsimd", lambda g: g.dma_start(out=tap_d["wsc"], in_=sv(300, 64)), "GD")
            tap_barrier(tb)

        r.wait("vector", stats_done)
        vop(lambda v: v.tensor_scalar_mul(out=scol(0), in0=PSX[:, 0:32], scalar1=1.0 / D))
        vop(lambda v: v.tensor_scalar_mul(out=scol(1), in0=PSY[:, 0:32], scalar1=1.0 / D))
        vop(lambda v: v.tensor_tensor(out=scol(7), in0=scol(0), in1=scol(0), op=ALU.mult))
        v_tok = vop(lambda v: v.tensor_sub(out=scol(1), in0=scol(1), in1=scol(7)))
        r.wait("scalar", v_tok)
        sop(lambda s: s.activation(out=scol(7), in_=scol(1), func=AF.Ln, bias=EPSC, scale=1.0))
        s_tok = sop(lambda s: s.activation(out=scol(2), in_=scol(7), func=AF.Exp, bias=0.0, scale=-0.5))
        r.wait("vector", s_tok)
        vop(lambda v: v.tensor_mul(out=scol(7), in0=scol(0), in1=scol(2)))
        vop(lambda v: v.tensor_scalar_mul(out=scol(3), in0=scol(7), scalar1=-1.0))
        vop(lambda v: v.tensor_mul(out=scol(7), in0=scol(2), in1=scol(2)))
        v2_tok = vop(lambda v: v.tensor_mul(out=scol(7), in0=scol(7), in1=scol(1)))
        r.wait("scalar", v2_tok)
        sop(lambda s: s.activation(out=XS2[:, 0:32], in_=scol(7), func=AF.Ln, bias=EPSC, scale=1.0))
        s2_tok = sop(lambda s: s.activation(out=scol(4), in_=XS2[:, 0:32], func=AF.Exp, bias=0.0, scale=-0.5))
        r.wait("vector", s2_tok)
        vop(lambda v: v.tensor_mul(out=scol(5), in0=scol(2), in1=scol(4)))
        ln_tok = vop(lambda v: v.tensor_mul(out=scol(6), in0=scol(3), in1=scol(4)))
        for slot, idx in ((0, 5), (1, 6), (2, 2), (3, 3)):
            vt_tok = None
            for bi in range(4):
                vt_tok = vop(lambda v, idx=idx, bi=bi: v.transpose(
                    out=TCOL[bi * 32:(bi + 1) * 32, 0:32],
                    in_=scol(idx)[bi * 32:(bi + 1) * 32, :]))
            r.wait("gpsimd", vt_tok)
            stk = None
            for bi in range(4):
                def mk(slot=slot, bi=bi):
                    def f(g):
                        dst = scr_rows[slot:slot + 1, :].rearrange(
                            "o (tt q p) -> o tt q p", q=4, p=32)[:, :, bi, :].squeeze(0)
                        return g.dma_start(out=dst, in_=TCOL[bi * 32:(bi + 1) * 32, 0:32])
                    return f
                stk = r.dma("gpsimd", mk(), "GD")
            r.wait("vector", stk)
        rows_tok = r.now("GD")
        if "stats" in taps:
            tb = r.dma("gpsimd", lambda g: g.dma_start(out=tap_d["stats"], in_=sv(SCOL, 256)), "GD")
            tap_barrier(tb)

        A2bc = fv(32768, 4096); B2bc = fv(36864, 4096)
        A1bc = fv(40960, 2048); B1bc = fv(43008, 2048)
        r.wait("gpsimd", rows_tok)
        r.dma("gpsimd", lambda g: g.dma_start(out=A2bc, in_=scr_rows[0:1, :].partition_broadcast(128)), "GD")
        r.dma("gpsimd", lambda g: g.dma_start(out=B2bc, in_=scr_rows[1:2, :].partition_broadcast(128)), "GD")
        r.dma("gpsimd", lambda g: g.dma_start(out=A1bc, in_=scr_rows[2:3, 0:NQ].partition_broadcast(128)), "GD")
        bc_tok = r.dma("gpsimd", lambda g: g.dma_start(out=B1bc, in_=scr_rows[3:4, 0:NQ].partition_broadcast(128)), "GD")

        tscr = fv(24576, 4096)
        tscr2 = fv(8192, 4096)
        tout = bvv(28672, 4096)
        r.wait("sync", wsbc_tok)
        r.wait("sync", wred_done)
        r.wait("vector", stats_done)
        tern_vtoks = [None] * 40
        tern_wtok = None
        for i in range(40):
            b = i % 2
            if i < 8:
                src = w1T[i * 128:(i + 1) * 128, :]; ncols = 4096
                dst = scr_W1q[i]; winv = WSC[:, 1:2]
            else:
                src = w2T[(i - 8) * 128:(i - 7) * 128, :]; ncols = 1024
                dst = scr_W2q[i - 8]; winv = WSC[:, 3:4]
            if i >= 2:
                r.wait("sync", tern_vtoks[i - 2])
            dtk = r.dma("sync", lambda s, src=src, b=b, ncols=ncols: s.dma_start(
                out=wchunk[b][:, 0:ncols], in_=src))
            r.wait("vector", dtk)
            if tern_wtok is not None:
                r.wait("vector", tern_wtok)
            r.op("vector", lambda v, b=b, ncols=ncols, winv=winv: v.tensor_scalar(
                out=tscr[:, 0:ncols], in0=wchunk[b][:, 0:ncols],
                scalar1=winv, scalar2=MAGIC, op0=ALU.mult, op1=ALU.add))
            r.op("vector", lambda v, ncols=ncols: v.tensor_scalar(
                out=tscr2[:, 0:ncols], in0=tscr[:, 0:ncols],
                scalar1=MAGIC, scalar2=1.0, op0=ALU.subtract, op1=ALU.min))
            tern_vtoks[i] = r.op("vector", lambda v, ncols=ncols: v.tensor_scalar_max(
                out=tout[:, 0:ncols], in0=tscr2[:, 0:ncols], scalar1=-1.0), "V")
            r.wait("gpsimd", tern_vtoks[i])
            tern_wtok = r.dma("gpsimd", lambda g, dst=dst, ncols=ncols: g.dma_start(
                out=dst, in_=tout[:, 0:ncols]), "GD")
        tern_done = tern_wtok
        if "w1q" in taps:
            r.wait("sync", tern_done)
            tw = r.dma("sync", lambda s: s.dma_start(out=wchunk[0].tensor.ap().bitcast(BF16)[:, 2 * 12288:2 * 12288 + 4096] if False else bvv(12288, 4096), in_=scr_W1q[0]))
            r.wait("vector", tw)
            tv = r.op("vector", lambda v: v.tensor_copy(out=tscr, in_=bvv(12288, 4096)), "V")
            r.wait("gpsimd", tv)
            tb = r.dma("gpsimd", lambda g: g.dma_start(out=tap_d["w1q"], in_=tscr), "GD")
            tap_barrier(tb)

        xchB = [fv(16384, 4096), fv(20480, 4096)]
        xnsc = fv(24576, 4096)
        xnown = fv(28672, 2048)
        Mx = fv(30720, 2048)
        XN2 = [bvv(dc * 2048, 4096) for dc in range(8)]

        r.wait("vector", bc_tok)
        r.op("vector", lambda v: v.memset(Mx, 0.0))
        act_tok = None
        mx_toks = [None] * 8
        for dc in range(8):
            b = dc % 2
            if dc >= 2:
                r.wait("sync", mx_toks[dc - 2])
            if dc == 0:
                r.wait("sync", tern_vtoks[39])
            dtk = r.dma("sync", lambda s, dc=dc, b=b: s.dma_start(
                out=xchB[b], in_=x_t[dc * 128:(dc + 1) * 128, :]))
            r.wait("vector", dtk)
            if act_tok is not None:
                r.wait("vector", act_tok)
            if dc == 0:
                r.wait("vector", tern_vtoks[39])
                r.wait("vector", tern_done)
            r.op("vector", lambda v, b=b: v.tensor_mul(out=xnsc, in0=xchB[b], in1=A2bc))
            xn_tok = r.op("vector", lambda v: v.tensor_add(out=xnsc, in0=xnsc, in1=B2bc), "V")
            r.op("vector", lambda v, b=b: v.tensor_mul(out=xnown, in0=xchB[b][:, 0:NQ], in1=A1bc))
            r.op("vector", lambda v: v.tensor_add(out=xnown, in0=xnown, in1=B1bc))
            r.op("vector", lambda v: v.scalar_tensor_tensor(
                out=Mx, in0=xnown, scalar=-1.0, in1=Mx, op0=ALU.mult, op1=ALU.max))
            mx_toks[dc] = r.op("vector", lambda v: v.tensor_max(out=Mx, in0=Mx, in1=xnown), "V")
            r.wait("scalar", xn_tok)
            act_tok = r.op("scalar", lambda s, dc=dc: s.activation(
                out=XN2[dc], in_=xnsc, func=AF.Identity,
                bias=LNB[:, dc:dc + 1], scale=LNW[:, dc:dc + 1]), "S")
        xn2_done = act_tok
        if "xn2" in taps:
            r.wait("vector", xn2_done)
            tv = r.op("vector", lambda v: v.tensor_copy(out=xnsc, in_=XN2[0]), "V")
            r.wait("gpsimd", tv)
            tgx = r.dma("gpsimd", lambda g: g.dma_start(out=tap_d["xn2"], in_=xnsc), "GD")
            tap_barrier(tgx)

        t64 = fv(24576, 2048, 0, 64)
        r.wait("vector", xn2_done)
        r.op("vector", lambda v: v.tensor_copy(out=t64[0:64, :], in_=Mx[64:128, :]))
        r.op("vector", lambda v: v.tensor_tensor(out=Mx[0:64, :], in0=Mx[0:64, :], in1=t64[0:64, :], op=ALU.max))
        r.op("vector", lambda v: v.tensor_copy(out=t64[0:32, :], in_=Mx[32:64, :]))
        r.op("vector", lambda v: v.tensor_tensor(out=Mx[0:32, :], in0=Mx[0:32, :], in1=t64[0:32, :], op=ALU.max))
        r.op("vector", lambda v: v.transpose(out=t64[0:32, :], in_=Mx[0:32, :]))
        xs_col = sv(SCOL, 64)
        xqf_col = sv(SCOL + 64, 64)
        xf1_col = sv(SCOL + 128, 64)
        vop(lambda v: v.tensor_reduce(
            out=XS2[0:32, :], in_=t64[0:32, :].rearrange("p (b f) -> p b f", f=32),
            axis=AX.X, op=ALU.max))
        vop(lambda v: v.tensor_scalar_max(out=xs_col[0:32, :], in0=XS2[0:32, :], scalar1=EPS))
        vop(lambda v: v.reciprocal(out=XS3[0:32, :], in_=xs_col[0:32, :]))
        vop(lambda v: v.tensor_scalar_mul(out=xqf_col[0:32, :], in0=XS3[0:32, :], scalar1=127.0))
        xf_tok = vop(lambda v: v.tensor_scalar(
            out=xf1_col[0:32, :], in0=xs_col[0:32, :],
            scalar1=WSC[0:32, 0:1], scalar2=1.0 / 127.0, op0=ALU.mult, op1=ALU.mult))
        for slot, col in ((4, xqf_col), (5, xf1_col)):
            vt_tok = None
            for bj in range(2):
                vt_tok = vop(lambda v, col=col, bj=bj: v.transpose(
                    out=TCOL[0:32, bj * 32:(bj + 1) * 32],
                    in_=col[0:32, bj * 32:(bj + 1) * 32]))
            r.wait("gpsimd", vt_tok)
            def mk2(slot=slot):
                def f(g):
                    dst = scr_rows[slot:slot + 1, 0:NQ].rearrange(
                        "o (j a b) -> o j a b", j=2, a=32, b=32).transpose(
                        [0, 2, 1, 3]).squeeze(0)
                    return g.dma_start(out=dst, in_=TCOL[0:32, 0:64].rearrange(
                        "a (j b) -> a j b", b=32))
                return f
            stk = r.dma("gpsimd", mk2(), "GD")
            r.wait("vector", stk)
        xq_rows_tok = r.now("GD")
        XQFbc = fv(30720, 2048)
        r.wait("gpsimd", xq_rows_tok)
        xqbc_tok = r.dma("gpsimd", lambda g: g.dma_start(
            out=XQFbc, in_=scr_rows[4:5, 0:NQ].partition_broadcast(128)), "GD")

        XQ = [bvv(16384 + dc * 1024, 2048) for dc in range(8)]
        xb2 = fv(24576, 2048)
        xn_t = fv(26624, 2048)
        xq_free_toks = [None] * 8
        xq_tok = None
        r.wait("sync", xf_tok)
        for dc in range(8):
            if dc >= 1:
                r.wait("sync", xq_free_toks[dc - 1])
            dtk = r.dma("sync", lambda s, dc=dc: s.dma_start(
                out=xb2, in_=x_t[dc * 128:(dc + 1) * 128, 0:NQ]))
            r.wait("vector", dtk)
            if dc == 0:
                r.wait("vector", xqbc_tok)
            r.op("vector", lambda v: v.tensor_mul(out=xn_t, in0=xb2, in1=A1bc))
            xq_free_toks[dc] = r.op("vector", lambda v: v.tensor_add(out=xn_t, in0=xn_t, in1=B1bc), "V")
            r.op("vector", lambda v: v.tensor_mul(out=xn_t, in0=xn_t, in1=XQFbc))
            xq_tok = r.op("vector", lambda v, dc=dc: v.tensor_scalar(
                out=XQ[dc], in0=xn_t, scalar1=MAGIC, scalar2=MAGIC,
                op0=ALU.add, op1=ALU.subtract), "V")
        xq_done = xq_tok
        if "xq" in taps:
            tv = r.op("vector", lambda v: v.tensor_copy(out=xn_t, in_=XQ[0]), "V")
            r.wait("gpsimd", tv)
            tgq = r.dma("gpsimd", lambda g: g.dma_start(out=tap_d["xq"], in_=xn_t), "GD")
            tap_barrier(tgq)

        CK = bvv(24576, 4096); SK = bvv(26624, 4096)
        CQ = bvv(28672, 2048); SQ = bvv(29696, 2048)
        trsc = fv(30720, 2048, 0, 32)
        Kpre = bvv(32768, 4096)
        swap = bvv(34816, 4096)
        m1 = bvv(36864, 2048); m2 = bvv(37888, 2048)
        Qpre = bvv(38912, 2048)
        V_fm = Kpre
        VTMa = bvv(34816, 2080)
        VTMb = bvv(36864, 2080)
        TSTG = [bvv(39936, 64), bvv(39968, 64)]
        WSTG = [fv(40960, 1024), fv(41984, 1024), fv(43008, 1024)]
        WB = [bvv(44032, 1024), bvv(44544, 1024), bvv(45056, 1024)]
        trsc2 = fv(47616, 2048, 0, 32)

        r.wait("sync", xq_done)
        ld_ck = r.dma("sync", lambda s: s.dma_start(out=CK, in_=ropeCK))
        ld_sk = r.dma("sync", lambda s: s.dma_start(out=SK, in_=ropeSK))
        ld_cq = r.dma("sync", lambda s: s.dma_start(out=CQ, in_=ropeCQ))
        ld_sq = r.dma("sync", lambda s: s.dma_start(out=SQ, in_=ropeSQ))
        for tok in (ld_ck, ld_sk, ld_cq, ld_sq):
            r.wait("vector", tok)
        r.wait("vector", xqbc_tok)

        kq_scr_tok = {}
        v_scr_tok = {}
        c_dma_tok = None
        prev_wcast = None
        prev_pair_scr = None

        def gen_fm(widx, bias_col, ntok, outbuf, first_mm_wait):
            evacs = []
            for ntc in range(ntok // 512):
                ps = PSX if ntc % 2 == 0 else PSY
                if ntc == 0 and first_mm_wait is not None:
                    r.wait("tensor", first_mm_wait)
                if len(evacs) >= 2:
                    r.wait("tensor", evacs[-2])
                tk = None
                for dc in range(8):
                    tk = r.op("tensor", lambda t, dc=dc, ntc=ntc, ps=ps, widx=widx: t.matmul(
                        out=ps[:, 0:512], lhsT=WB[widx][:, dc * 128:(dc + 1) * 128],
                        rhs=XN2[dc][:, ntc * 512:(ntc + 1) * 512],
                        start=(dc == 0), stop=(dc == 7)), "T" if dc == 7 else None)
                r.wait("scalar", tk)
                evacs.append(r.op("scalar", lambda s, ntc=ntc, ps=ps, bias_col=bias_col, outbuf=outbuf: s.activation(
                    out=outbuf[:, ntc * 512:(ntc + 1) * 512], in_=ps[:, 0:512],
                    func=AF.Identity, bias=bias_col, scale=1.0), "S"))
            return evacs[-1]

        for p in range(NPAIR):
            if prev_wcast is not None:
                r.wait("sync", prev_wcast)
            wtoks = []
            for wi, wsrc in enumerate((wqT, wkT, wvT)):
                def mkw(wi=wi, wsrc=wsrc, p=p):
                    def f(s):
                        src = wsrc[:, p * 128:(p + 1) * 128].rearrange("(dc q) f -> q dc f", q=128)
                        return s.dma_start(out=WSTG[wi].rearrange("q (dc f) -> q dc f", f=128), in_=src)
                    return f
                wtoks.append(r.dma("sync", mkw()))
            casts = []
            for wi in range(3):
                r.wait("vector", wtoks[wi])
                casts.append(r.op("vector", lambda v, wi=wi: v.tensor_copy(out=WB[wi], in_=WSTG[wi]), "V"))
            prev_wcast = casts[-1]

            k_evac = gen_fm(1, BKC[:, p:p + 1], SEQ, Kpre, prev_wcast)
            r.wait("vector", k_evac)
            for h4 in range(4):
                src = (32, 0, 96, 64)[h4]
                r.op("vector", lambda v, h4=h4, src=src: v.tensor_copy(
                    out=swap[h4 * 32:(h4 + 1) * 32, :], in_=Kpre[src:src + 32, :]))
            for hh in range(2):
                sl = slice(hh * 2048, (hh + 1) * 2048)
                r.op("vector", lambda v, sl=sl: v.tensor_tensor(out=m1, in0=Kpre[:, sl], in1=CK[:, sl], op=ALU.mult))
                r.op("vector", lambda v, sl=sl: v.tensor_tensor(out=m2, in0=swap[:, sl], in1=SK[:, sl], op=ALU.mult))
                r.op("vector", lambda v, sl=sl: v.tensor_add(out=Kpre[:, sl], in0=m1, in1=m2))
            krope_tok = r.op("vector", lambda v: v.engine_nop(), "V")

            q_evac = gen_fm(0, BQC[:, p:p + 1], NQ, Qpre, krope_tok)
            r.wait("vector", q_evac)
            for h4 in range(4):
                src = (32, 0, 96, 64)[h4]
                r.op("vector", lambda v, h4=h4, src=src: v.tensor_copy(
                    out=swap[h4 * 32:(h4 + 1) * 32, 0:2048], in_=Qpre[src:src + 32, :]))
            r.op("vector", lambda v: v.tensor_tensor(out=m1, in0=Qpre, in1=CQ, op=ALU.mult))
            r.op("vector", lambda v: v.tensor_tensor(out=m2, in0=swap[:, 0:2048], in1=SQ, op=ALU.mult))
            qrope_tok = r.op("vector", lambda v: v.tensor_add(out=Qpre, in0=m1, in1=m2), "V")

            ctok = None
            for h01 in range(2):
                r.wait("tensor", qrope_tok)
                if ctok is not None:
                    r.wait("tensor", ctok)
                stok = None
                for qq in range(4):
                    stok = r.op("tensor", lambda t, h01=h01, qq=qq: t.matmul(
                        out=PS_BIG[0:64, qq * 512:(qq + 1) * 512],
                        lhsT=Kpre[h01 * 64:h01 * 64 + 64, 32:4096:64],
                        rhs=Qpre[h01 * 64:h01 * 64 + 64, qq * 512:(qq + 1) * 512],
                        start=True, stop=True, tile_position=(h01 * 64, 0)),
                        "T" if qq == 3 else None)
                r.wait("vector", stok)
                vop(lambda v: v.tensor_copy(out=trsc2[0:32, :], in_=PS_BIG[32:64, :]))
                vop(lambda v: v.tensor_tensor(
                    out=trsc[0:32, :], in0=PS_BIG[0:32, :], in1=trsc2[0:32, :], op=ALU.max))
                vop(lambda v: v.transpose(out=trsc2[0:32, :], in_=trsc[0:32, :]))
                vop(lambda v: v.memset(NEGC, -1e30))
                vop(lambda v: v.tensor_reduce(
                    out=NEGC[0:32, 0:4],
                    in_=trsc2[0:32, :].rearrange("p (qq bf) -> p qq bf", qq=4),
                    axis=AX.X, op=ALU.max))
                vop(lambda v: v.transpose(out=NEGCT, in_=NEGC))
                vop(lambda v: v.tensor_reduce(out=CF2, in_=NEGCT, axis=AX.X, op=ALU.max))
                ctok = vop(lambda v: v.tensor_scalar_mul(out=CFIN, in0=CF2, scalar1=-1.0))
                r.wait("gpsimd", ctok)
                head = 2 * p + h01
                c_dma_tok = r.dma("gpsimd", lambda g, head=head: g.dma_start(
                    out=scr_c[0:1, head * 4:head * 4 + 4], in_=CFIN[0:4, 0:1]), "GD")

            r.wait("gpsimd", qrope_tok)
            kq_scr_tok[p] = [
                r.dma("gpsimd", lambda g, p=p: g.dma_start(out=scr_K[p], in_=Kpre), "GD"),
                r.dma("gpsimd", lambda g, p=p: g.dma_start(out=scr_Q[p], in_=Qpre), "GD"),
            ]

            v_evac = gen_fm(2, BVC[:, p:p + 1], SEQ, V_fm, kq_scr_tok[p][1])
            r.wait("vector", v_evac)
            o65a = r.op("vector", lambda v: v.memset(
                VTMa.rearrange("q (tt c) -> q tt c", c=65)[:, :, 64:65], 1.0))
            o65b = r.op("vector", lambda v: v.memset(
                VTMb.rearrange("q (tt c) -> q tt c", c=65)[:, :, 64:65], 1.0), "V")
            r.wait("scalar", v_evac)
            if prev_pair_scr is not None:
                r.wait("scalar", prev_pair_scr)
            tcopies = []
            for h01 in range(2):
                vtm = VTMa if h01 == 0 else VTMb
                for tt in range(32):
                    sl = TSTG[tt % 2]
                    idx = h01 * 32 + tt
                    if idx >= 2:
                        r.wait("scalar", tcopies[idx - 2])
                    ttok = r.dma("scalar", lambda s, h01=h01, tt=tt, sl=sl: s.dma_start_transpose(
                        out=sl, in_=V_fm[h01 * 64:(h01 + 1) * 64, tt * 128:(tt + 1) * 128]), "SD")
                    r.wait("vector", ttok)
                    if idx == 0:
                        r.wait("vector", o65b)
                    tcopies.append(r.op("vector", lambda v, vtm=vtm, tt=tt, sl=sl: v.tensor_copy(
                        out=vtm[:, tt * 65:tt * 65 + 64], in_=sl), "V"))
                head = 2 * p + h01
                r.wait("gpsimd", tcopies[-1])
                def mkv(head=head, vtm=vtm):
                    def f(g):
                        return g.dma_start(out=scr_V[head].transpose([1, 0, 2]),
                                           in_=vtm.rearrange("q (tt c) -> q tt c", c=65))
                    return f
                v_scr_tok[head] = r.dma("gpsimd", mkv(), "GD")
            prev_pair_scr = v_scr_tok[2 * p + 1]

        c_all_done = prev_pair_scr
        r.wait("gpsimd", c_dma_tok)
        cbias_tok = r.dma("gpsimd", lambda g: g.dma_start(
            out=CBIAS, in_=scr_c[0:1, 0:64].partition_broadcast(128)), "GD")
        if "cneg" in taps:
            r.wait("gpsimd", cbias_tok)
            tb = r.dma("gpsimd", lambda g: g.dma_start(out=tap_d["cneg"], in_=scr_c), "GD")
            tap_barrier(tb)

        KB = [bvv(0, 4096), bvv(2048, 4096)]
        QB = [bvv(4096, 2048), bvv(5120, 2048)]
        VA = [bvv(6144, 2080), bvv(8224, 2080)]
        VB = [bvv(7184, 2080), bvv(9264, 2080)]
        Pb = [bvv(10304, 2048), bvv(11328, 2048)]
        PVS_A = fv(12352, 2048, 0, 64)
        PVS_B = fv(32768, 2048, 0, 64)
        RLBC_A = fv(34816, 2048, 0, 64)
        RLBC_B = fv(36864, 2048, 0, 64)
        LR_A = fv(38912, 2048, 0, 1)
        LRr_A = fv(38912, 2048, 32, 33)
        LR_B = fv(38912, 2048, 64, 65)
        LRr_B = fv(38912, 2048, 96, 97)
        BST = bvv(40960, 2048)
        ATT = [bvv(24576 + pp * 1024, 2048) for pp in range(NPAIR)]

        def load_pair(p, parity, buf_free_tok):
            r.wait("sync", v_scr_tok[2 * p + 1])
            r.wait("sync", c_all_done)
            if buf_free_tok is not None:
                r.wait("sync", buf_free_tok)
            return [
                r.dma("sync", lambda s, p=p, parity=parity: s.dma_start(out=KB[parity], in_=scr_K[p])),
                r.dma("sync", lambda s, p=p, parity=parity: s.dma_start(out=QB[parity], in_=scr_Q[p])),
                r.dma("sync", lambda s, p=p, parity=parity: s.dma_start(
                    out=VA[parity].rearrange("q (tt c) -> q tt c", c=65),
                    in_=scr_V[2 * p].transpose([1, 0, 2]))),
                r.dma("sync", lambda s, p=p, parity=parity: s.dma_start(
                    out=VB[parity].rearrange("q (tt c) -> q tt c", c=65),
                    in_=scr_V[2 * p + 1].transpose([1, 0, 2]))),
            ]

        r.wait("tensor", cbias_tok)
        pair_load_toks = load_pair(0, 0, None)
        pair_last_T = {}
        pb_free = [None, None]
        sc_free = [None, None]
        last_exp = None
        bank_free = {0: None, 1: None}
        prev_att_mults = None
        att_done_tok = None

        for p in range(NPAIR):
            parity = p % 2
            if p + 1 < NPAIR:
                next_load = load_pair(p + 1, 1 - parity, pair_last_T.get(p - 1))
            for tok in pair_load_toks:
                r.wait("tensor", tok)
            evac_v_tok = None
            for qc in range(4):
                PVq = [PV0, PV1] if qc % 2 == 0 else [PSX, PSY]
                if bank_free[qc % 2] is not None:
                    r.wait("tensor", bank_free[qc % 2])
                for h01 in range(2):
                    PVps = PVq[h01]
                    ci = (2 * p + h01) * 4 + qc
                    pv_tok = None
                    exp_toks = [None] * 17
                    for u in range(17):
                        half = u % 2
                        if u < 16:
                            r.wait("tensor", sc_free[half])
                            sc_tok = None
                            for kt in range(2):
                                ki = u * 2 + kt
                                sc_tok = r.op("tensor", lambda t, h01=h01, kt=kt, ki=ki, qc=qc, parity=parity, hbase=half * 1024: t.matmul(
                                    out=PS_BIG[:, hbase + kt * 512:hbase + (kt + 1) * 512],
                                    lhsT=KB[parity][h01 * 64:(h01 + 1) * 64, ki * 128:(ki + 1) * 128],
                                    rhs=QB[parity][h01 * 64:(h01 + 1) * 64, qc * 512:(qc + 1) * 512],
                                    start=True, stop=True, tile_position=(h01 * 64, 0)),
                                    "T" if kt == 1 else None)
                            r.wait("scalar", sc_tok)
                            if pb_free[half] is not None:
                                r.wait("scalar", pb_free[half])
                            exp_toks[u] = r.op("scalar", lambda s, slot=half, ci=ci, hbase=half * 1024: s.activation(
                                out=Pb[slot][:, 0:1024], in_=PS_BIG[:, hbase:hbase + 1024], func=AF.Exp,
                                bias=CBIAS[:, ci:ci + 1], scale=1.0), "S")
                            sc_free[half] = exp_toks[u]
                        if u >= 1:
                            up = u - 1
                            slot = up % 2
                            r.wait("tensor", exp_toks[up])
                            for kt in range(2):
                                ki = up * 2 + kt
                                vbuf = VA[parity] if h01 == 0 else VB[parity]
                                pv_tok = r.op("tensor", lambda t, PVps=PVps, vbuf=vbuf, ki=ki, kt=kt, slot=slot: t.matmul(
                                    out=PVps[0:65, :],
                                    lhsT=vbuf[:, ki * 65:ki * 65 + 65],
                                    rhs=Pb[slot][:, kt * 512:(kt + 1) * 512],
                                    start=(ki == 0), stop=(ki == 31)),
                                    "T" if kt == 1 else None)
                            pb_free[slot] = pv_tok
                    r.wait("vector", pv_tok)
                    PVS = PVS_A if h01 == 0 else PVS_B
                    LR = LR_A if h01 == 0 else LR_B
                    if prev_att_mults is not None and qc == 0 and h01 == 0:
                        r.wait("vector", prev_att_mults)
                    r.op("vector", lambda v, PVS=PVS, qc=qc, PVps=PVps: v.tensor_copy(
                        out=PVS[0:64, qc * 512:(qc + 1) * 512], in_=PVps[0:64, :]))
                    evac_v_tok = r.op("vector", lambda v, LR=LR, qc=qc, PVps=PVps: v.tensor_copy(
                        out=LR[:, qc * 512:(qc + 1) * 512], in_=PVps[64:65, :]), "V")
                bank_free[qc % 2] = evac_v_tok
            r.op("vector", lambda v: v.reciprocal(out=LRr_A, in_=LR_A))
            rl_tok = r.op("vector", lambda v: v.reciprocal(out=LRr_B, in_=LR_B), "V")
            r.wait("gpsimd", rl_tok)
            r.dma("gpsimd", lambda g: g.dma_start(out=scr_rows[7:8, 0:NQ], in_=LRr_A), "GD")
            w2tk = r.dma("gpsimd", lambda g: g.dma_start(out=scr_rows[8:9, 0:NQ], in_=LRr_B), "GD")
            r.wait("gpsimd", w2tk)
            r.dma("gpsimd", lambda g: g.dma_start(out=RLBC_A, in_=scr_rows[7:8, 0:NQ].partition_broadcast(64)), "GD")
            rb_tok = r.dma("gpsimd", lambda g: g.dma_start(out=RLBC_B, in_=scr_rows[8:9, 0:NQ].partition_broadcast(64)), "GD")
            r.wait("vector", rb_tok)
            r.op("vector", lambda v, p=p: v.tensor_tensor(
                out=ATT[p][0:64, :], in0=PVS_A[0:64, :], in1=RLBC_A[0:64, :], op=ALU.mult))
            r.op("vector", lambda v: v.tensor_tensor(
                out=BST[0:64, :], in0=PVS_B[0:64, :], in1=RLBC_B[0:64, :], op=ALU.mult))
            prev_att_mults = r.op("vector", lambda v, p=p: v.tensor_copy(
                out=ATT[p][64:128, :], in_=BST[0:64, :]), "V")
            pair_last_T[p] = r.now("T")
            if p + 1 < NPAIR:
                pair_load_toks = next_load
        att_done_tok = prev_att_mults
        if "att" in taps:
            r.wait("vector", att_done_tok)
            tvv = r.op("vector", lambda v: v.tensor_copy(out=PVS_A[0:64, :], in_=ATT[0][0:64, :]))
            tvv = r.op("vector", lambda v: v.tensor_copy(out=fv(12352, 2048, 64, 128), in_=ATT[0][64:128, :]), "V")
            r.wait("gpsimd", tvv)
            tg = r.dma("gpsimd", lambda g: g.dma_start(out=tap_d["att"], in_=fv(12352, 2048)), "GD")
            tap_barrier(tg)
        if "pvs" in taps:
            r.wait("gpsimd", att_done_tok)
            tb = r.dma("gpsimd", lambda g: g.dma_start(out=tap_d["pvs"], in_=PVS_B[0:64, :]), "GD")
            tap_barrier(tb)
        if "lr" in taps:
            r.wait("gpsimd", att_done_tok)
            tb = r.dma("gpsimd", lambda g: g.dma_start(out=tap_d["lr"], in_=LR_B), "GD")
            tap_barrier(tb)

        H = [fv(ht * 256, 256) for ht in range(32)]
        HQ = [bvv(8192 + ht * 128, 256) for ht in range(32)]
        Mh = fv(12288, 256)
        W1t = [bvv(12544, 1024), bvv(13056, 1024)]
        W2t = [bvv(13568, 1024), bvv(14080, 1024)]
        DQ = [fv(14592, 256), fv(14848, 256)]
        TR1 = fv(15104, 256, 0, 64)
        TR2 = fv(15360, 256, 0, 32)
        XF1bc = fv(40960, 2048)
        QFbc = fv(43008, 256)
        HF2bc = fv(43264, 256)
        hs_col = sv(SCOL, 8)
        qf_col = sv(SCOL + 64, 8)
        hf2_col = sv(SCOL + 128, 8)

        r.wait("gpsimd", att_done_tok)
        xf1bc_tok = r.dma("gpsimd", lambda g: g.dma_start(
            out=XF1bc, in_=scr_rows[5:6, 0:NQ].partition_broadcast(128)), "GD")
        r.wait("sync", att_done_tok)
        r.wait("sync", tern_done)

        gelu_slot = [None, None]
        deq_slot = [None, None]
        mm1_slot = [None, None]
        hq_prev_tc = None
        mm2_last_tc = None
        g_done = None
        for tc in range(8):
            csl = slice(tc * 256, (tc + 1) * 256)
            r.wait("vector", mm2_last_tc)
            mh_tok = r.op("vector", lambda v: v.memset(Mh, 0.0), "V")
            for ht in range(32):
                wslot = ht % 2
                wb = W1t[wslot]
                if mm1_slot[wslot] is not None:
                    r.wait("sync", mm1_slot[wslot])
                def mk_w1(ht=ht, wb=wb):
                    def f(s):
                        src = scr_W1q[:, :, ht * 128:(ht + 1) * 128].transpose([1, 0, 2])
                        return s.dma_start(out=wb.rearrange("q (dc f) -> q dc f", f=128), in_=src)
                    return f
                dtk = r.dma("sync", mk_w1())
                r.wait("tensor", dtk)
                if tc == 0 and ht == 0:
                    r.wait("tensor", xq_done)
                ps = PSX if ht % 2 == 0 else PSY
                if deq_slot[wslot] is not None:
                    r.wait("tensor", deq_slot[wslot])
                mmtok = None
                for dc in range(8):
                    mmtok = r.op("tensor", lambda t, dc=dc, wb=wb, ps=ps, csl=csl: t.matmul(
                        out=ps[:, 0:256], lhsT=wb[:, dc * 128:(dc + 1) * 128],
                        rhs=XQ[dc][:, csl], start=(dc == 0), stop=(dc == 7)),
                        "T" if dc == 7 else None)
                mm1_slot[wslot] = mmtok
                r.wait("vector", mmtok)
                if tc == 0 and ht == 0:
                    r.wait("vector", xf1bc_tok)
                dq = DQ[wslot]
                if gelu_slot[wslot] is not None:
                    r.wait("vector", gelu_slot[wslot])
                if hq_prev_tc is not None and ht < 2:
                    r.wait("scalar", hq_prev_tc)
                deq_slot[wslot] = r.op("vector", lambda v, dq=dq, ps=ps, csl=csl: v.tensor_tensor(
                    out=dq, in0=ps[:, 0:256], in1=XF1bc[:, csl], op=ALU.mult), "V")
                r.wait("scalar", deq_slot[wslot])
                gelu_slot[wslot] = r.op("scalar", lambda s, ht=ht, dq=dq: s.activation(
                    out=H[ht], in_=dq, func=AF.Gelu, bias=B1C[:, ht:ht + 1], scale=1.0), "S")
                r.wait("vector", gelu_slot[wslot])
                r.op("vector", lambda v, ht=ht: v.scalar_tensor_tensor(
                    out=Mh, in0=H[ht], scalar=-1.0, in1=Mh, op0=ALU.mult, op1=ALU.max))
                r.op("vector", lambda v, ht=ht: v.tensor_max(out=Mh, in0=Mh, in1=H[ht]))
            r.op("vector", lambda v: v.tensor_copy(out=TR1[0:64, :], in_=Mh[64:128, :]))
            r.op("vector", lambda v: v.tensor_tensor(out=Mh[0:64, :], in0=Mh[0:64, :], in1=TR1[0:64, :], op=ALU.max))
            r.op("vector", lambda v: v.tensor_copy(out=TR2[0:32, :], in_=Mh[32:64, :]))
            r.op("vector", lambda v: v.tensor_tensor(out=Mh[0:32, :], in0=Mh[0:32, :], in1=TR2[0:32, :], op=ALU.max))
            r.op("vector", lambda v: v.transpose(out=TR2[0:32, :], in_=Mh[0:32, :]))
            vop(lambda v: v.tensor_reduce(
                out=XS2[0:32, 0:8], in_=TR2[0:32, :].rearrange("p (b f) -> p b f", f=32),
                axis=AX.X, op=ALU.max))
            vop(lambda v: v.tensor_scalar_max(out=hs_col[0:32, :], in0=XS2[0:32, 0:8], scalar1=EPS))
            vop(lambda v: v.reciprocal(out=XS3[0:32, 0:8], in_=hs_col[0:32, :]))
            vop(lambda v: v.tensor_scalar_mul(out=qf_col[0:32, :], in0=XS3[0:32, 0:8], scalar1=127.0))
            hf_tok = vop(lambda v: v.tensor_scalar(
                out=hf2_col[0:32, :], in0=hs_col[0:32, :],
                scalar1=WSC[0:32, 2:3], scalar2=1.0 / 127.0, op0=ALU.mult, op1=ALU.mult))
            for slot, col in ((9, qf_col), (10, hf2_col)):
                colw = sv(SCOL + 64, 32) if col is qf_col else sv(SCOL + 128, 32)
                vt_tok = vop(lambda v, colw=colw: v.transpose(
                    out=TCOL[0:32, 96:128], in_=colw[0:32, :]))
                r.wait("gpsimd", vt_tok)
                def mk3(slot=slot, tc=tc):
                    def f(g):
                        dst = scr_rows[slot:slot + 1, tc * 256:(tc + 1) * 256].rearrange(
                            "o (b p) -> o b p", p=32).squeeze(0)
                        return g.dma_start(out=dst, in_=TCOL[0:8, 96:128])
                    return f
                stk = r.dma("gpsimd", mk3(), "GD")
                r.wait("vector", stk)
            hrow_tok = r.now("GD")
            r.wait("gpsimd", hrow_tok)
            r.dma("gpsimd", lambda g, tc=tc: g.dma_start(
                out=QFbc, in_=scr_rows[9:10, tc * 256:(tc + 1) * 256].partition_broadcast(128)), "GD")
            hbc_tok = r.dma("gpsimd", lambda g, tc=tc: g.dma_start(
                out=HF2bc, in_=scr_rows[10:11, tc * 256:(tc + 1) * 256].partition_broadcast(128)), "GD")
            r.wait("vector", hbc_tok)
            hq_last = None
            for ht in range(32):
                r.op("vector", lambda v, ht=ht: v.tensor_mul(out=H[ht], in0=H[ht], in1=QFbc))
                hq_last = r.op("vector", lambda v, ht=ht: v.tensor_scalar(
                    out=HQ[ht], in0=H[ht], scalar1=MAGIC, scalar2=MAGIC,
                    op0=ALU.add, op1=ALU.subtract), "V" if ht == 31 else None)
            hq_prev_tc = hq_last
            if "h0" in taps and tc == 0:
                r.wait("gpsimd", hq_last)
                tb = r.dma("gpsimd", lambda g: g.dma_start(out=tap_d["h0"], in_=H[0]), "GD")
                tap_barrier(tb)
            mm2_tok = None
            sil_slot = [None, None]
            deq2_slot = [None, None]
            w2_dma_slot = [None, None]
            for dt in range(8):
                psl = dt % 2
                ps = PSX if psl == 0 else PSY
                if deq2_slot[psl] is not None:
                    r.wait("tensor", deq2_slot[psl])
                for grp in range(4):
                    gslot = grp % 2
                    wb = W2t[gslot]
                    if w2_dma_slot[gslot] is not None:
                        r.wait("sync", w2_dma_slot[gslot])
                    def mk_w2(dt=dt, grp=grp, wb=wb):
                        def f(s):
                            src = scr_W2q[grp * 8:(grp + 1) * 8, :, dt * 128:(dt + 1) * 128].transpose([1, 0, 2])
                            return s.dma_start(out=wb.rearrange("q (hc f) -> q hc f", f=128), in_=src)
                        return f
                    d2 = r.dma("sync", mk_w2())
                    r.wait("tensor", d2)
                    if dt == 0 and grp == 0:
                        r.wait("tensor", hq_last)
                    for j in range(8):
                        hc = grp * 8 + j
                        mm2_tok = r.op("tensor", lambda t, j=j, hc=hc, wb=wb, ps=ps: t.matmul(
                            out=ps[:, 0:256], lhsT=wb[:, j * 128:(j + 1) * 128],
                            rhs=HQ[hc], start=(hc == 0), stop=(hc == 31)),
                            "T" if j == 7 else None)
                    w2_dma_slot[gslot] = mm2_tok
                r.wait("vector", mm2_tok)
                dq = DQ[psl]
                if sil_slot[psl] is not None:
                    r.wait("vector", sil_slot[psl])
                deq2_slot[psl] = r.op("vector", lambda v, dq=dq, ps=ps: v.tensor_tensor(
                    out=dq, in0=ps[:, 0:256], in1=HF2bc, op=ALU.mult), "V")
                r.wait("scalar", deq2_slot[psl])
                sil_slot[psl] = r.op("scalar", lambda s, dt=dt, dq=dq, csl=csl: s.activation(
                    out=bvv(32768 + dt * 1024, 2048)[:, csl], in_=dq,
                    func=AF.Silu, bias=B2C[:, dt:dt + 1], scale=1.0), "S")
            mm2_last_tc = r.now("T")
            g_done = sil_slot[(8 - 1) % 2]
        if "g0" in taps:
            r.wait("vector", g_done)
            tvg = r.op("vector", lambda v: v.tensor_copy(out=fv(16384, 2048), in_=bvv(32768, 2048)), "V")
            r.wait("gpsimd", tvg)
            tgg = r.dma("gpsimd", lambda g: g.dma_start(out=tap_d["g0"], in_=fv(16384, 2048)), "GD")
            tap_barrier(tgg)

        G = [bvv(32768 + dt * 1024, 2048) for dt in range(8)]

        s_chunk = fv(0, 2048)
        xn_chunk = fv(2048, 2048)
        o_chunk = fv(4096, 2048)
        xF = fv(6144, 2048)
        wo_stage = fv(8192, 1024)
        WOB = [bvv(9216 + ac * 512, 1024) for ac in range(8)]
        A1b2 = fv(13312, 2048)
        B1b2 = fv(15360, 2048)

        r.wait("gpsimd", g_done)
        r.dma("gpsimd", lambda g: g.dma_start(out=A1b2, in_=scr_rows[2:3, 0:NQ].partition_broadcast(128)), "GD")
        fbc_tok = r.dma("gpsimd", lambda g: g.dma_start(out=B1b2, in_=scr_rows[3:4, 0:NQ].partition_broadcast(128)), "GD")

        r.wait("sync", g_done)
        wo_cast = None
        for ac in range(8):
            if wo_cast is not None:
                r.wait("sync", wo_cast)
            dtk = r.dma("sync", lambda s, ac=ac: s.dma_start(
                out=wo_stage, in_=woT[ac * 128:(ac + 1) * 128, :]))
            r.wait("vector", dtk)
            wo_cast = r.op("vector", lambda v, ac=ac: v.tensor_copy(out=WOB[ac], in_=wo_stage), "V")

        st_count = 0
        prev_out_v = None
        prev_evacs = []
        last_psum_user_v = r.now("V")
        for dt in range(8):
            if prev_out_v is not None:
                r.wait("sync", prev_out_v)
            xtk = r.dma("sync", lambda s, dt=dt: s.dma_start(
                out=xF, in_=x_t[dt * 128:(dt + 1) * 128, 0:NQ]))
            evac_toks = []
            for qc in range(4):
                ps = PSX if qc % 2 == 0 else PSY
                if dt == 0 and qc == 0:
                    r.wait("tensor", wo_cast)
                    r.wait("tensor", last_psum_user_v)
                idx = dt * 4 + qc
                allev = prev_evacs + evac_toks
                if idx >= 2:
                    r.wait("tensor", allev[idx - 2])
                wo_mm = None
                for ac in range(8):
                    wo_mm = r.op("tensor", lambda t, ac=ac, qc=qc, dt=dt, ps=ps: t.matmul(
                        out=ps[:, 0:512], lhsT=WOB[ac][:, dt * 128:(dt + 1) * 128],
                        rhs=ATT[ac][:, qc * 512:(qc + 1) * 512],
                        start=(ac == 0), stop=(ac == 7)), "T" if ac == 7 else None)
                r.wait("scalar", wo_mm)
                if dt > 0 and qc == 0:
                    r.wait("scalar", prev_out_v)
                ev = r.op("scalar", lambda s, qc=qc, dt=dt, ps=ps: s.activation(
                    out=s_chunk[:, qc * 512:(qc + 1) * 512], in_=ps[:, 0:512],
                    func=AF.Identity, bias=BOC[:, dt:dt + 1], scale=1.0), "S")
                evac_toks.append(ev)
            prev_evacs = prev_evacs + evac_toks
            r.wait("vector", evac_toks[-1])
            r.wait("vector", xtk)
            if dt == 0:
                r.wait("vector", fbc_tok)
            if prev_out_v is not None:
                r.wait("vector", ("ST", st_count))
            r.op("vector", lambda v: v.tensor_mul(out=xn_chunk, in0=xF, in1=A1b2))
            r.op("vector", lambda v: v.tensor_add(out=xn_chunk, in0=xn_chunk, in1=B1b2))
            r.op("vector", lambda v: v.tensor_add(out=s_chunk, in0=s_chunk, in1=xn_chunk))
            r.op("vector", lambda v, dt=dt: v.tensor_tensor(out=o_chunk, in0=s_chunk, in1=G[dt], op=ALU.mult))
            prev_out_v = r.op("vector", lambda v: v.tensor_add(out=o_chunk, in0=o_chunk, in1=xF), "V")
            r.wait("sync", prev_out_v)
            r.dma("sync", lambda s, dt=dt: s.dma_start(
                out=out_d[dt * 128:(dt + 1) * 128, :], in_=o_chunk), "ST")
            st_count += 16
        r.wait("sync", ("ST", st_count))

        for name in ("sync", "gpsimd", "tensor", "vector", "scalar"):
            items = r.q[name]
            getattr(block, name)(lambda eng, items=items: _replay(eng, items, sems))

    return nc


_NC_CACHE = {}


def _get_nc(taps=()):
    key = tuple(sorted(taps))
    if key not in _NC_CACHE:
        _NC_CACHE[key] = build(taps)
    return _NC_CACHE[key]


def _rope_tables():
    i = np.arange(0, 64, 2, dtype=np.float32) / np.float32(64.0)
    inv_freq = (np.float32(1.0) / (np.float32(10000.0) ** i)).astype(np.float32)
    t = np.arange(SEQ, dtype=np.float32)
    ang = np.outer(t, inv_freq).astype(np.float32)
    cos = np.cos(ang).astype(np.float32).T
    sin = np.sin(ang).astype(np.float32).T
    C = np.tile(cos, (4, 1))
    S = np.concatenate([-sin, sin, -sin, sin], axis=0)
    return C, S


def make_in_maps(x, ln_w, ln_b, Wq, bq, Wk, bk, Wv, bv, Wo, bo, W1, b1, W2, b2):
    perm = np.concatenate([h * 64 + np.concatenate([np.arange(0, 64, 2), np.arange(1, 64, 2)])
                           for h in range(NHEAD)]).astype(np.int64)
    asc = np.ascontiguousarray
    f32 = np.float32
    bf = ml_dtypes.bfloat16
    wqT_ = asc(np.asarray(Wq, f32).T[:, perm])
    wkT_ = asc(np.asarray(Wk, f32).T[:, perm])
    wvT_ = asc(np.asarray(Wv, f32).T)
    woT_ = asc(np.asarray(Wo, f32).T)
    col8 = lambda a: asc(np.asarray(a, f32).reshape(8, 128).T)
    col32 = lambda a: asc(np.asarray(a, f32).reshape(32, 128).T)
    shared = dict(
        wqT=wqT_, wkT=wkT_, wvT=wvT_, woT=woT_,
        bqc=col8(np.asarray(bq, f32)[perm]), bkc=col8(np.asarray(bk, f32)[perm]),
        bvc=col8(bv), boc=col8(bo), lnwc=col8(ln_w), lnbc=col8(ln_b),
        b1c=col32(b1), b2c=col8(b2),
        w1T=asc(np.asarray(W1, f32).T), w2T=asc(np.asarray(W2, f32).T),
    )
    C, S = _rope_tables()
    in_maps = []
    for core in range(8):
        b = core // 2
        qlo = (core % 2) * NQ
        olo = NQ - qlo
        order = np.concatenate([np.arange(qlo, qlo + NQ), np.arange(olo, olo + NQ)])
        xT = np.asarray(x[b], f32).T
        m = dict(shared)
        m["x_t"] = asc(xT[:, order])
        m["ropeCK"] = asc(C[:, order]).astype(bf)
        m["ropeSK"] = asc(S[:, order]).astype(bf)
        m["ropeCQ"] = (asc(C[:, qlo:qlo + NQ]) / np.float32(8.0)).astype(bf)
        m["ropeSQ"] = (asc(S[:, qlo:qlo + NQ]) / np.float32(8.0)).astype(bf)
        in_maps.append(m)
    return in_maps


def kernel(L0_attn_x, L0_attn_ln_w, L0_attn_ln_b, L0_attn_Wq, L0_attn_bq,
           L0_attn_Wk, L0_attn_bk, L0_attn_Wv, L0_attn_bv, L0_attn_Wo,
           L0_attn_bo, L0_ffn_W1, L0_ffn_b1, L0_ffn_W2, L0_ffn_b2):
    in_maps = make_in_maps(L0_attn_x, L0_attn_ln_w, L0_attn_ln_b, L0_attn_Wq,
                           L0_attn_bq, L0_attn_Wk, L0_attn_bk, L0_attn_Wv,
                           L0_attn_bv, L0_attn_Wo, L0_attn_bo, L0_ffn_W1,
                           L0_ffn_b1, L0_ffn_W2, L0_ffn_b2)
    nc = _get_nc()
    res = run_bass_kernel_spmd(nc, in_maps, core_ids=list(range(8)))
    out = np.empty((4, SEQ, D), dtype=np.float32)
    for core in range(8):
        b = core // 2
        qlo = (core % 2) * NQ
        out[b, qlo:qlo + NQ, :] = np.asarray(res.results[core]["out"], dtype=np.float32).T
    return out


# version 08
# speedup vs baseline: 1.1509x; 1.1509x over previous
import numpy as np
import ml_dtypes
import concourse.bass as bass
import concourse.mybir as mybir
from concourse.bass_utils import run_bass_kernel_spmd
from contextlib import ExitStack

F32 = mybir.dt.float32
BF16 = mybir.dt.bfloat16
AF = mybir.ActivationFunctionType
ALU = mybir.AluOpType
AX = mybir.AxisListType

D = 1024
SEQ = 4096
NQ = 2048
HFF = 4096
NHEAD = 16
NPAIR = 8
MAGIC = 12582912.0
EPS = 1e-5

SEMS = ("LD", "GD", "SD", "T", "V", "S", "ST")


class Rec:
    def __init__(self):
        self.q = {k: [] for k in ("sync", "gpsimd", "tensor", "vector", "scalar")}
        self.c = {k: 0 for k in SEMS}

    def op(self, eng, fn, inc=None, k=1):
        self.q[eng].append(("op", fn, inc, k))
        if inc is not None:
            self.c[inc] += k
            return (inc, self.c[inc])
        return None

    def dma(self, eng, fn, inc="LD"):
        return self.op(eng, fn, inc, 16)

    def wait(self, eng, tok):
        if tok is not None:
            self.q[eng].append(("wait", tok[0], tok[1]))

    def now(self, sem):
        return (sem, self.c[sem])


def _replay(engine, items, sems):
    for it in items:
        if it[0] == "wait":
            engine.wait_ge(sems[it[1]], it[2])
        else:
            ins = it[1](engine)
            if it[2] is not None:
                ins.then_inc(sems[it[2]], it[3])


AR_N = 47616
SM_N = 896


def build(taps=()):
    taps = set(taps)
    nc = bass.Bass("TRN2", target_bir_lowering=False, debug=False)

    def din(name, shape, dt=F32):
        return nc.dram_tensor(name, shape, dt, kind="ExternalInput").ap()

    x_t = din("x_t", [D, SEQ])
    wqT = din("wqT", [D, D]); wkT = din("wkT", [D, D])
    wvT = din("wvT", [D, D]); woT = din("woT", [D, D])
    bqc = din("bqc", [128, 8]); bkc = din("bkc", [128, 8])
    bvc = din("bvc", [128, 8]); boc = din("boc", [128, 8])
    lnwc = din("lnwc", [128, 8]); lnbc = din("lnbc", [128, 8])
    b1c = din("b1c", [128, 32]); b2c = din("b2c", [128, 8])
    w1T = din("w1T", [D, HFF]); w2T = din("w2T", [HFF, D])
    ropeCK = din("ropeCK", [128, SEQ], BF16); ropeSK = din("ropeSK", [128, SEQ], BF16)
    ropeCQ = din("ropeCQ", [128, NQ], BF16); ropeSQ = din("ropeSQ", [128, NQ], BF16)

    out_d = nc.dram_tensor("out", [D, NQ], F32, kind="ExternalOutput").ap()

    def dscr(name, shape, dt):
        return nc.dram_tensor(name, shape, dt, kind="Internal").ap()

    scr_rows = dscr("scr_rows", [16, SEQ], F32)
    scr_c = dscr("scr_c", [1, 256], F32)
    scr_K = dscr("scr_K", [NPAIR, 128, SEQ], BF16)
    scr_Q = dscr("scr_Q", [NPAIR, 128, NQ], BF16)
    scr_V = dscr("scr_V", [16, 32, 128, 65], BF16)
    scr_W1q = dscr("scr_W1q", [32, 128, 1024], BF16)
    scr_W2q = dscr("scr_W2q", [8, 4, 128, 1024], BF16)

    TAPDEFS = {
        "stats": ([128, 256], F32), "xn2": ([128, SEQ], F32), "xq": ([128, NQ], F32),
        "cneg": ([1, 256], F32), "att": ([128, NQ], F32), "g0": ([128, NQ], F32),
        "w1q": ([128, HFF], F32), "h0": ([128, 512], F32), "pvs": ([64, NQ], F32),
        "wsc": ([128, 64], F32),
        "lr": ([1, NQ], F32),
    }
    tap_d = {t: nc.dram_tensor("tap_" + t, TAPDEFS[t][0], TAPDEFS[t][1],
                               kind="ExternalOutput").ap() for t in taps}

    r = Rec()

    with ExitStack() as ctx:
        e = ctx.enter_context
        AR = e(nc.sbuf_tensor("AR", [128, AR_N], F32))
        SM = e(nc.sbuf_tensor("SM", [128, SM_N], F32))
        SM_BIG = e(nc.sbuf_tensor("SM_BIG", [128, 4096], F32))
        SM_BIG_B = SM_BIG.ap().bitcast(BF16)
        AR_B = AR.ap().bitcast(BF16)
        SM_B = SM.ap().bitcast(BF16)

        def fv(off, n, p0=0, p1=128):
            return AR[p0:p1, off:off + n]

        def bvv(off, nb, p0=0, p1=128):
            return AR_B[p0:p1, 2 * off:2 * off + nb]

        def sv(off, n, p0=0, p1=128):
            return SM[p0:p1, off:off + n]

        ones_bf = SM_B[:, 0:1]
        ones_f = sv(1, 1)
        SCOL = 8
        def scol(i, n=32):
            return sv(SCOL + 32 * i, n)
        W1S = sv(300, 8); W2S = sv(308, 32)
        WSC = sv(340, 4)
        BQC = sv(360, 8); BKC = sv(368, 8); BVC = sv(376, 8); BOC = sv(384, 8)
        LNW = sv(392, 8); LNB = sv(400, 8)
        B1C = sv(408, 32); B2C = sv(440, 8)
        CBIAS = sv(448, 64)
        NEGC = sv(512, 32, 0, 32)
        NEGCT = sv(544, 32, 0, 32)
        CFIN = sv(576, 1, 0, 32)
        WS11 = sv(578, 1, 0, 1); WS12 = sv(579, 1, 0, 1)
        WS21 = sv(580, 1, 0, 1); WS22 = sv(581, 1, 0, 1)
        W1T_ = sv(584, 1); W2T_ = sv(585, 1)
        EPSC = sv(586, 1)
        TCOL = sv(608, 128)
        XS2 = sv(736, 64)
        XS3 = sv(800, 64)
        CF2 = sv(592, 1, 0, 32)

        PS_BIG = e(nc.psum_tensor("PS_BIG", [128, 2048], F32))
        PV0 = e(nc.psum_tensor("PV0", [128, 512], F32))
        PV1 = e(nc.psum_tensor("PV1", [128, 512], F32))
        PSX = e(nc.psum_tensor("PSX", [128, 512], F32))
        PSY = e(nc.psum_tensor("PSY", [128, 512], F32))

        sems = {k: e(nc.semaphore(k)) for k in SEMS}
        block = e(nc.Block())

        def tap_barrier(tok):
            for _eng in ("sync", "vector", "scalar", "tensor", "gpsimd"):
                r.wait(_eng, tok)

        VOPT = 1

        def vop(fn):
            if VOPT == 0:
                return r.op("vector", fn, "V")
            if VOPT == 2:
                tok = r.op("vector", fn, "V")
                r.op("vector", lambda v: v.engine_nop())
                r.op("vector", lambda v: v.engine_nop())
                return tok
            tok = r.op("vector", fn, "V")
            r.wait("vector", tok)
            return tok

        def sop(fn):
            if VOPT == 0:
                return r.op("scalar", fn, "S")
            if VOPT == 2:
                tok = r.op("scalar", fn, "S")
                r.op("scalar", lambda x: x.engine_nop())
                r.op("scalar", lambda x: x.engine_nop())
                return tok
            tok = r.op("scalar", fn, "S")
            r.wait("scalar", tok)
            return tok


        xchunk = [fv(0, 4096), fv(4096, 4096)]
        xbuf = bvv(8192, 4096)
        sqbuf = bvv(10240, 4096)
        wchunk = [fv(12288, 4096), fv(16384, 4096)]

        r.op("vector", lambda v: v.memset(ones_bf, 1.0))
        r.op("vector", lambda v: v.memset(ones_f, 1.0))
        r.op("vector", lambda v: v.memset(EPSC, EPS), "V")
        for dst, src in ((BQC, bqc), (BKC, bkc), (BVC, bvc), (BOC, boc),
                         (LNW, lnwc), (LNB, lnbc), (B1C, b1c), (B2C, b2c)):
            r.dma("gpsimd", lambda g, dst=dst, src=src: g.dma_start(out=dst, in_=src), "GD")

        cast_toks = [None] * 8
        stats_mm_tok = None
        r.wait("tensor", r.now("V"))
        r.op("tensor", lambda t: t.matmul(out=PSX[:, 500:501], lhsT=SM_B[:, 0:128], rhs=ones_bf, start=True, stop=True))
        r.op("tensor", lambda t: t.matmul(out=PSY[:, 500:501], lhsT=SM_B[:, 0:128], rhs=ones_bf, start=True, stop=True))
        wred_toks = [None] * 40
        for dc in range(8):
            b = dc % 2
            if dc >= 2:
                r.wait("sync", cast_toks[dc - 2])
            xd = r.dma("sync", lambda s, dc=dc, b=b: s.dma_start(
                out=xchunk[b], in_=x_t[dc * 128:(dc + 1) * 128, :]))
            r.wait("vector", xd)
            if stats_mm_tok is not None:
                r.wait("vector", stats_mm_tok)
            r.op("vector", lambda v, b=b: v.tensor_copy(out=xbuf, in_=xchunk[b]))
            cast_toks[dc] = r.op("vector", lambda v: v.tensor_tensor(
                out=sqbuf, in0=xbuf, in1=xbuf, op=ALU.mult), "V")
            r.wait("tensor", cast_toks[dc])
            for tt in range(32):
                r.op("tensor", lambda t, tt=tt, dc=dc: t.matmul(
                    out=PSX[:, tt:tt + 1], lhsT=xbuf[:, tt * 128:(tt + 1) * 128],
                    rhs=ones_bf, start=False, stop=True))
            for tt in range(32):
                stats_mm_tok = r.op("tensor", lambda t, tt=tt, dc=dc: t.matmul(
                    out=PSY[:, tt:tt + 1], lhsT=sqbuf[:, tt * 128:(tt + 1) * 128],
                    rhs=ones_bf, start=False, stop=True),
                    "T" if tt == 31 else None)
        stats_done = stats_mm_tok

        for i in range(40):
            b = i % 2
            if i >= 2:
                r.wait("gpsimd", wred_toks[i - 2])
            if i < 8:
                wd = r.dma("gpsimd", lambda g, i=i, b=b: g.dma_start(
                    out=wchunk[b], in_=w1T[i * 128:(i + 1) * 128, :]), "GD")
                fn = lambda v, i=i, b=b: v.tensor_reduce(
                    out=W1S[:, i:i + 1], in_=wchunk[b], axis=AX.X, op=ALU.add,
                    apply_absolute_value=True)
            else:
                wd = r.dma("gpsimd", lambda g, i=i, b=b: g.dma_start(
                    out=wchunk[b][:, 0:1024], in_=w2T[(i - 8) * 128:(i - 7) * 128, :]), "GD")
                fn = lambda v, i=i, b=b: v.tensor_reduce(
                    out=W2S[:, i - 8:i - 7], in_=wchunk[b][:, 0:1024], axis=AX.X,
                    op=ALU.add, apply_absolute_value=True)
            r.wait("vector", wd)
            wred_toks[i] = r.op("vector", fn, "V")
        wred_done = wred_toks[39]

        r.op("vector", lambda v: v.tensor_reduce(out=W1T_, in_=W1S, axis=AX.X, op=ALU.add))
        tot_tok = r.op("vector", lambda v: v.tensor_reduce(out=W2T_, in_=W2S, axis=AX.X, op=ALU.add), "V")
        r.wait("tensor", tot_tok)
        r.op("tensor", lambda t: t.matmul(out=PSX[0:1, 40:41], lhsT=W1T_, rhs=ones_f, start=False, stop=True))
        wtot_mm = r.op("tensor", lambda t: t.matmul(out=PSX[0:1, 41:42], lhsT=W2T_, rhs=ones_f, start=False, stop=True), "T")
        r.wait("vector", wtot_mm)
        vop(lambda v: v.tensor_scalar(out=WS11, in0=PSX[0:1, 40:41], scalar1=1.0 / (D * HFF), scalar2=EPS, op0=ALU.mult, op1=ALU.max))
        vop(lambda v: v.tensor_scalar(out=WS21, in0=PSX[0:1, 41:42], scalar1=1.0 / (D * HFF), scalar2=EPS, op0=ALU.mult, op1=ALU.max))
        vop(lambda v: v.reciprocal(out=WS12, in_=WS11))
        ws_tok = vop(lambda v: v.reciprocal(out=WS22, in_=WS21))
        r.wait("gpsimd", ws_tok)
        r.dma("gpsimd", lambda g: g.dma_start(out=scr_rows[6:7, 0:1], in_=WS11), "GD")
        r.dma("gpsimd", lambda g: g.dma_start(out=scr_rows[6:7, 1:2], in_=WS12), "GD")
        r.dma("gpsimd", lambda g: g.dma_start(out=scr_rows[6:7, 2:3], in_=WS21), "GD")
        t4 = r.dma("gpsimd", lambda g: g.dma_start(out=scr_rows[6:7, 3:4], in_=WS22), "GD")
        r.wait("gpsimd", t4)
        wsbc_tok = r.dma("gpsimd", lambda g: g.dma_start(
            out=WSC, in_=scr_rows[6:7, 0:4].partition_broadcast(128)), "GD")
        if "wsc" in taps:
            r.wait("gpsimd", wsbc_tok)
            tb = r.dma("gpsimd", lambda g: g.dma_start(out=tap_d["wsc"], in_=sv(300, 64)), "GD")
            tap_barrier(tb)

        r.wait("vector", stats_done)
        vop(lambda v: v.tensor_scalar_mul(out=scol(0), in0=PSX[:, 0:32], scalar1=1.0 / D))
        vop(lambda v: v.tensor_scalar_mul(out=scol(1), in0=PSY[:, 0:32], scalar1=1.0 / D))
        vop(lambda v: v.tensor_tensor(out=scol(7), in0=scol(0), in1=scol(0), op=ALU.mult))
        v_tok = vop(lambda v: v.tensor_sub(out=scol(1), in0=scol(1), in1=scol(7)))
        r.wait("scalar", v_tok)
        sop(lambda s: s.activation(out=scol(7), in_=scol(1), func=AF.Ln, bias=EPSC, scale=1.0))
        s_tok = sop(lambda s: s.activation(out=scol(2), in_=scol(7), func=AF.Exp, bias=0.0, scale=-0.5))
        r.wait("vector", s_tok)
        vop(lambda v: v.tensor_mul(out=scol(7), in0=scol(0), in1=scol(2)))
        vop(lambda v: v.tensor_scalar_mul(out=scol(3), in0=scol(7), scalar1=-1.0))
        vop(lambda v: v.tensor_mul(out=scol(7), in0=scol(2), in1=scol(2)))
        v2_tok = vop(lambda v: v.tensor_mul(out=scol(7), in0=scol(7), in1=scol(1)))
        r.wait("scalar", v2_tok)
        sop(lambda s: s.activation(out=XS2[:, 0:32], in_=scol(7), func=AF.Ln, bias=EPSC, scale=1.0))
        s2_tok = sop(lambda s: s.activation(out=scol(4), in_=XS2[:, 0:32], func=AF.Exp, bias=0.0, scale=-0.5))
        r.wait("vector", s2_tok)
        vop(lambda v: v.tensor_mul(out=scol(5), in0=scol(2), in1=scol(4)))
        ln_tok = vop(lambda v: v.tensor_mul(out=scol(6), in0=scol(3), in1=scol(4)))
        for slot, idx in ((0, 5), (1, 6), (2, 2), (3, 3)):
            vt_tok = None
            for bi in range(4):
                vt_tok = vop(lambda v, idx=idx, bi=bi: v.transpose(
                    out=TCOL[bi * 32:(bi + 1) * 32, 0:32],
                    in_=scol(idx)[bi * 32:(bi + 1) * 32, :]))
            r.wait("gpsimd", vt_tok)
            stk = None
            for bi in range(4):
                def mk(slot=slot, bi=bi):
                    def f(g):
                        dst = scr_rows[slot:slot + 1, :].rearrange(
                            "o (tt q p) -> o tt q p", q=4, p=32)[:, :, bi, :].squeeze(0)
                        return g.dma_start(out=dst, in_=TCOL[bi * 32:(bi + 1) * 32, 0:32])
                    return f
                stk = r.dma("gpsimd", mk(), "GD")
            r.wait("vector", stk)
        rows_tok = r.now("GD")
        if "stats" in taps:
            tb = r.dma("gpsimd", lambda g: g.dma_start(out=tap_d["stats"], in_=sv(SCOL, 256)), "GD")
            tap_barrier(tb)

        A2bc = fv(32768, 4096); B2bc = fv(36864, 4096)
        A1bc = fv(40960, 2048); B1bc = fv(43008, 2048)
        r.wait("gpsimd", rows_tok)
        r.dma("gpsimd", lambda g: g.dma_start(out=A2bc, in_=scr_rows[0:1, :].partition_broadcast(128)), "GD")
        r.dma("gpsimd", lambda g: g.dma_start(out=B2bc, in_=scr_rows[1:2, :].partition_broadcast(128)), "GD")
        r.dma("gpsimd", lambda g: g.dma_start(out=A1bc, in_=scr_rows[2:3, 0:NQ].partition_broadcast(128)), "GD")
        bc_tok = r.dma("gpsimd", lambda g: g.dma_start(out=B1bc, in_=scr_rows[3:4, 0:NQ].partition_broadcast(128)), "GD")

        tscr = fv(24576, 4096)
        tscr2 = fv(8192, 4096)
        tout2 = [bvv(28672, 4096), bvv(30720, 4096)]
        r.wait("sync", wsbc_tok)
        r.wait("sync", wred_done)
        r.wait("vector", stats_done)
        tern_vtoks = [None] * 40
        tern_wtok = None
        tern_wtoks = [None] * 40
        for i in range(40):
            b = i % 2
            if i < 8:
                src = w1T[i * 128:(i + 1) * 128, :]; ncols = 4096
                dc = i
                dst = scr_W1q.rearrange("h p (dcc f) -> h p dcc f", f=128)[:, :, dc, :].transpose([1, 0, 2])
                winv = WSC[:, 1:2]
            else:
                src = w2T[(i - 8) * 128:(i - 7) * 128, :]; ncols = 1024
                hc = i - 8
                dst = scr_W2q.rearrange("d g p (j f) -> d g p j f", f=128)[:, hc // 8, :, hc % 8, :].transpose([1, 0, 2])
                winv = WSC[:, 3:4]
            if i >= 2:
                r.wait("sync", tern_vtoks[i - 2])
            dtk = r.dma("sync", lambda s, src=src, b=b, ncols=ncols: s.dma_start(
                out=wchunk[b][:, 0:ncols], in_=src))
            r.wait("vector", dtk)
            if i >= 2 and tern_wtoks[i - 2] is not None:
                r.wait("vector", tern_wtoks[i - 2])
            r.op("vector", lambda v, b=b, ncols=ncols, winv=winv: v.tensor_scalar(
                out=tscr[:, 0:ncols], in0=wchunk[b][:, 0:ncols],
                scalar1=winv, scalar2=MAGIC, op0=ALU.mult, op1=ALU.add))
            r.op("vector", lambda v, ncols=ncols: v.tensor_scalar(
                out=tscr2[:, 0:ncols], in0=tscr[:, 0:ncols],
                scalar1=MAGIC, scalar2=1.0, op0=ALU.subtract, op1=ALU.min))
            tern_vtoks[i] = r.op("vector", lambda v, ncols=ncols, b=b: v.tensor_scalar_max(
                out=tout2[b][:, 0:ncols], in0=tscr2[:, 0:ncols], scalar1=-1.0), "V")
            r.wait("gpsimd", tern_vtoks[i])
            tern_wtoks[i] = r.dma("gpsimd", lambda g, dst=dst, ncols=ncols, b=b: g.dma_start(
                out=dst, in_=tout2[b][:, 0:ncols]), "GD")
        tern_done = tern_wtoks[39]
        if "w1q" in taps:
            r.wait("sync", tern_done)
            tw = r.dma("sync", lambda s: s.dma_start(
                out=bvv(12288, 4096),
                in_=scr_W1q.rearrange("h p (dcc f) -> h p dcc f", f=128)[:, :, 0, :].transpose([1, 0, 2])))
            r.wait("vector", tw)
            tv = r.op("vector", lambda v: v.tensor_copy(out=tscr, in_=bvv(12288, 4096)), "V")
            r.wait("gpsimd", tv)
            tb = r.dma("gpsimd", lambda g: g.dma_start(out=tap_d["w1q"], in_=tscr), "GD")
            tap_barrier(tb)

        xchB = [fv(16384, 4096), fv(20480, 4096)]
        xnsc = fv(24576, 4096)
        xnown = fv(28672, 2048)
        Mx = fv(30720, 2048)
        XN2 = [bvv(dc * 2048, 4096) for dc in range(8)]

        r.wait("vector", bc_tok)
        r.wait("vector", tern_vtoks[39])
        r.wait("vector", tern_done)
        r.op("vector", lambda v: v.memset(Mx, 0.0))
        act_tok = None
        mx_toks = [None] * 8
        for dc in range(8):
            b = dc % 2
            if dc >= 2:
                r.wait("sync", mx_toks[dc - 2])
            if dc == 0:
                r.wait("sync", tern_vtoks[39])
            dtk = r.dma("sync", lambda s, dc=dc, b=b: s.dma_start(
                out=xchB[b], in_=x_t[dc * 128:(dc + 1) * 128, :]))
            r.wait("vector", dtk)
            if act_tok is not None:
                r.wait("vector", act_tok)
            if dc == 0:
                r.wait("vector", tern_vtoks[39])
                r.wait("vector", tern_done)
            r.op("vector", lambda v, b=b: v.tensor_mul(out=xnsc, in0=xchB[b], in1=A2bc))
            xn_tok = r.op("vector", lambda v: v.tensor_add(out=xnsc, in0=xnsc, in1=B2bc), "V")
            r.op("vector", lambda v, b=b: v.tensor_mul(out=xnown, in0=xchB[b][:, 0:NQ], in1=A1bc))
            r.op("vector", lambda v: v.tensor_add(out=xnown, in0=xnown, in1=B1bc))
            r.op("vector", lambda v: v.scalar_tensor_tensor(
                out=Mx, in0=xnown, scalar=-1.0, in1=Mx, op0=ALU.mult, op1=ALU.max))
            mx_toks[dc] = r.op("vector", lambda v: v.tensor_max(out=Mx, in0=Mx, in1=xnown), "V")
            r.wait("scalar", xn_tok)
            act_tok = r.op("scalar", lambda s, dc=dc: s.activation(
                out=XN2[dc], in_=xnsc, func=AF.Identity,
                bias=LNB[:, dc:dc + 1], scale=LNW[:, dc:dc + 1]), "S")
        xn2_done = act_tok
        if "xn2" in taps:
            r.wait("vector", xn2_done)
            tv = r.op("vector", lambda v: v.tensor_copy(out=xnsc, in_=XN2[0]), "V")
            r.wait("gpsimd", tv)
            tgx = r.dma("gpsimd", lambda g: g.dma_start(out=tap_d["xn2"], in_=xnsc), "GD")
            tap_barrier(tgx)

        t64 = fv(24576, 2048, 0, 64)
        r.wait("vector", xn2_done)
        r.op("vector", lambda v: v.tensor_copy(out=t64[0:64, :], in_=Mx[64:128, :]))
        r.op("vector", lambda v: v.tensor_tensor(out=Mx[0:64, :], in0=Mx[0:64, :], in1=t64[0:64, :], op=ALU.max))
        r.op("vector", lambda v: v.tensor_copy(out=t64[0:32, :], in_=Mx[32:64, :]))
        r.op("vector", lambda v: v.tensor_tensor(out=Mx[0:32, :], in0=Mx[0:32, :], in1=t64[0:32, :], op=ALU.max))
        r.op("vector", lambda v: v.transpose(out=t64[0:32, :], in_=Mx[0:32, :]))
        xs_col = sv(SCOL, 64)
        xqf_col = sv(SCOL + 64, 64)
        xf1_col = sv(SCOL + 128, 64)
        vop(lambda v: v.tensor_reduce(
            out=XS2[0:32, :], in_=t64[0:32, :].rearrange("p (b f) -> p b f", f=32),
            axis=AX.X, op=ALU.max))
        vop(lambda v: v.tensor_scalar_max(out=xs_col[0:32, :], in0=XS2[0:32, :], scalar1=EPS))
        vop(lambda v: v.reciprocal(out=XS3[0:32, :], in_=xs_col[0:32, :]))
        vop(lambda v: v.tensor_scalar_mul(out=xqf_col[0:32, :], in0=XS3[0:32, :], scalar1=127.0))
        xf_tok = vop(lambda v: v.tensor_scalar(
            out=xf1_col[0:32, :], in0=xs_col[0:32, :],
            scalar1=WSC[0:32, 0:1], scalar2=1.0 / 127.0, op0=ALU.mult, op1=ALU.mult))
        for slot, col in ((4, xqf_col), (5, xf1_col)):
            vt_tok = None
            for bj in range(2):
                vt_tok = vop(lambda v, col=col, bj=bj: v.transpose(
                    out=TCOL[0:32, bj * 32:(bj + 1) * 32],
                    in_=col[0:32, bj * 32:(bj + 1) * 32]))
            r.wait("gpsimd", vt_tok)
            def mk2(slot=slot):
                def f(g):
                    dst = scr_rows[slot:slot + 1, 0:NQ].rearrange(
                        "o (j a b) -> o j a b", j=2, a=32, b=32).transpose(
                        [0, 2, 1, 3]).squeeze(0)
                    return g.dma_start(out=dst, in_=TCOL[0:32, 0:64].rearrange(
                        "a (j b) -> a j b", b=32))
                return f
            stk = r.dma("gpsimd", mk2(), "GD")
            r.wait("vector", stk)
        xq_rows_tok = r.now("GD")
        XQFbc = fv(30720, 2048)
        r.wait("gpsimd", xq_rows_tok)
        xqbc_tok = r.dma("gpsimd", lambda g: g.dma_start(
            out=XQFbc, in_=scr_rows[4:5, 0:NQ].partition_broadcast(128)), "GD")

        XQ = [bvv(16384 + dc * 1024, 2048) for dc in range(8)]
        xb2 = fv(24576, 2048)
        xn_t = fv(26624, 2048)
        xq_free_toks = [None] * 8
        xq_tok = None
        r.wait("sync", xf_tok)
        for dc in range(8):
            if dc >= 1:
                r.wait("sync", xq_free_toks[dc - 1])
            dtk = r.dma("sync", lambda s, dc=dc: s.dma_start(
                out=xb2, in_=x_t[dc * 128:(dc + 1) * 128, 0:NQ]))
            r.wait("vector", dtk)
            if dc == 0:
                r.wait("vector", xqbc_tok)
            r.op("vector", lambda v: v.tensor_mul(out=xn_t, in0=xb2, in1=A1bc))
            xq_free_toks[dc] = r.op("vector", lambda v: v.tensor_add(out=xn_t, in0=xn_t, in1=B1bc), "V")
            r.op("vector", lambda v: v.tensor_mul(out=xn_t, in0=xn_t, in1=XQFbc))
            xq_tok = r.op("vector", lambda v, dc=dc: v.tensor_scalar(
                out=XQ[dc], in0=xn_t, scalar1=MAGIC, scalar2=MAGIC,
                op0=ALU.add, op1=ALU.subtract), "V")
        xq_done = xq_tok
        if "xq" in taps:
            tv = r.op("vector", lambda v: v.tensor_copy(out=xn_t, in_=XQ[0]), "V")
            r.wait("gpsimd", tv)
            tgq = r.dma("gpsimd", lambda g: g.dma_start(out=tap_d["xq"], in_=xn_t), "GD")
            tap_barrier(tgq)

        CK = bvv(24576, 4096); SK = bvv(26624, 4096)
        CQ = bvv(28672, 2048); SQ = bvv(29696, 2048)
        trsc = fv(30720, 2048, 0, 32)
        Kpre = bvv(32768, 4096)
        swap = bvv(34816, 4096)
        m1 = bvv(36864, 2048); m2 = bvv(37888, 2048)
        Qpre = bvv(38912, 2048)
        V_fm = Kpre
        VTMa = bvv(34816, 2080)
        VTMb = bvv(36864, 2080)
        TSTG = [bvv(39936, 64), bvv(39968, 64)]
        WSTG = [fv(40960, 1024), fv(41984, 1024), fv(43008, 1024)]
        WB = [bvv(44032, 1024), bvv(44544, 1024), bvv(45056, 1024)]
        trsc2 = fv(45568, 2048, 0, 32)

        r.wait("sync", xq_done)
        ld_ck = r.dma("sync", lambda s: s.dma_start(out=CK, in_=ropeCK))
        ld_sk = r.dma("sync", lambda s: s.dma_start(out=SK, in_=ropeSK))
        ld_cq = r.dma("sync", lambda s: s.dma_start(out=CQ, in_=ropeCQ))
        ld_sq = r.dma("sync", lambda s: s.dma_start(out=SQ, in_=ropeSQ))
        for tok in (ld_ck, ld_sk, ld_cq, ld_sq):
            r.wait("vector", tok)
        r.wait("vector", xqbc_tok)

        kq_scr_tok = {}
        v_scr_tok = {}
        c_dma_tok = None
        prev_wcast = None
        prev_pair_scr = None

        def gen_fm(widx, bias_col, ntok, outbuf, first_mm_wait):
            evacs = []
            for ntc in range(ntok // 512):
                ps = PSX if ntc % 2 == 0 else PSY
                if ntc == 0 and first_mm_wait is not None:
                    r.wait("tensor", first_mm_wait)
                if len(evacs) >= 2:
                    r.wait("tensor", evacs[-2])
                tk = None
                for dc in range(8):
                    tk = r.op("tensor", lambda t, dc=dc, ntc=ntc, ps=ps, widx=widx: t.matmul(
                        out=ps[:, 0:512], lhsT=WB[widx][:, dc * 128:(dc + 1) * 128],
                        rhs=XN2[dc][:, ntc * 512:(ntc + 1) * 512],
                        start=(dc == 0), stop=(dc == 7)), "T" if dc == 7 else None)
                r.wait("scalar", tk)
                evacs.append(r.op("scalar", lambda s, ntc=ntc, ps=ps, bias_col=bias_col, outbuf=outbuf: s.activation(
                    out=outbuf[:, ntc * 512:(ntc + 1) * 512], in_=ps[:, 0:512],
                    func=AF.Identity, bias=bias_col, scale=1.0), "S"))
            return evacs[-1]

        for p in range(NPAIR):
            if prev_wcast is not None:
                r.wait("sync", prev_wcast)
            wtoks = []
            for wi, wsrc in enumerate((wqT, wkT, wvT)):
                def mkw(wi=wi, wsrc=wsrc, p=p):
                    def f(s):
                        src = wsrc[:, p * 128:(p + 1) * 128].rearrange("(dc q) f -> q dc f", q=128)
                        return s.dma_start(out=WSTG[wi].rearrange("q (dc f) -> q dc f", f=128), in_=src)
                    return f
                wtoks.append(r.dma("sync", mkw()))
            casts = []
            for wi in range(3):
                r.wait("vector", wtoks[wi])
                casts.append(r.op("vector", lambda v, wi=wi: v.tensor_copy(out=WB[wi], in_=WSTG[wi]), "V"))
            prev_wcast = casts[-1]

            k_evac = gen_fm(1, BKC[:, p:p + 1], SEQ, Kpre, prev_wcast)
            r.wait("vector", k_evac)
            for h4 in range(4):
                src = (32, 0, 96, 64)[h4]
                r.op("vector", lambda v, h4=h4, src=src: v.tensor_copy(
                    out=swap[h4 * 32:(h4 + 1) * 32, :], in_=Kpre[src:src + 32, :]))
            for hh in range(2):
                sl = slice(hh * 2048, (hh + 1) * 2048)
                r.op("vector", lambda v, sl=sl: v.tensor_tensor(out=m1, in0=Kpre[:, sl], in1=CK[:, sl], op=ALU.mult))
                r.op("vector", lambda v, sl=sl: v.tensor_tensor(out=m2, in0=swap[:, sl], in1=SK[:, sl], op=ALU.mult))
                r.op("vector", lambda v, sl=sl: v.tensor_add(out=Kpre[:, sl], in0=m1, in1=m2))
            krope_tok = r.op("vector", lambda v: v.engine_nop(), "V")

            q_evac = gen_fm(0, BQC[:, p:p + 1], NQ, Qpre, krope_tok)
            r.wait("vector", q_evac)
            for h4 in range(4):
                src = (32, 0, 96, 64)[h4]
                r.op("vector", lambda v, h4=h4, src=src: v.tensor_copy(
                    out=swap[h4 * 32:(h4 + 1) * 32, 0:2048], in_=Qpre[src:src + 32, :]))
            r.op("vector", lambda v: v.tensor_tensor(out=m1, in0=Qpre, in1=CQ, op=ALU.mult))
            r.op("vector", lambda v: v.tensor_tensor(out=m2, in0=swap[:, 0:2048], in1=SQ, op=ALU.mult))
            qrope_tok = r.op("vector", lambda v: v.tensor_add(out=Qpre, in0=m1, in1=m2), "V")

            ctok = None
            for h01 in range(2):
                r.wait("tensor", qrope_tok)
                if ctok is not None:
                    r.wait("tensor", ctok)
                stok = None
                for qq in range(4):
                    stok = r.op("tensor", lambda t, h01=h01, qq=qq: t.matmul(
                        out=PS_BIG[0:64, qq * 512:(qq + 1) * 512],
                        lhsT=Kpre[h01 * 64:h01 * 64 + 64, 32:4096:64],
                        rhs=Qpre[h01 * 64:h01 * 64 + 64, qq * 512:(qq + 1) * 512],
                        start=True, stop=True, tile_position=(h01 * 64, 0)),
                        "T" if qq == 3 else None)
                r.wait("vector", stok)
                vop(lambda v: v.tensor_copy(out=trsc2[0:32, :], in_=PS_BIG[32:64, :]))
                vop(lambda v: v.tensor_tensor(
                    out=trsc[0:32, :], in0=PS_BIG[0:32, :], in1=trsc2[0:32, :], op=ALU.max))
                vop(lambda v: v.transpose(out=trsc2[0:32, :], in_=trsc[0:32, :]))
                vop(lambda v: v.memset(NEGC, -1e30))
                vop(lambda v: v.tensor_reduce(
                    out=NEGC[0:32, 0:4],
                    in_=trsc2[0:32, :].rearrange("p (qq bf) -> p qq bf", qq=4),
                    axis=AX.X, op=ALU.max))
                vop(lambda v: v.transpose(out=NEGCT, in_=NEGC))
                vop(lambda v: v.tensor_reduce(out=CF2, in_=NEGCT, axis=AX.X, op=ALU.max))
                ctok = vop(lambda v: v.tensor_scalar_mul(out=CFIN, in0=CF2, scalar1=-1.0))
                r.wait("gpsimd", ctok)
                head = 2 * p + h01
                c_dma_tok = r.dma("gpsimd", lambda g, head=head: g.dma_start(
                    out=scr_c[0:1, head * 4:head * 4 + 4], in_=CFIN[0:4, 0:1]), "GD")

            r.wait("gpsimd", qrope_tok)
            kq_scr_tok[p] = [
                r.dma("gpsimd", lambda g, p=p: g.dma_start(out=scr_K[p], in_=Kpre), "GD"),
                r.dma("gpsimd", lambda g, p=p: g.dma_start(out=scr_Q[p], in_=Qpre), "GD"),
            ]

            v_evac = gen_fm(2, BVC[:, p:p + 1], SEQ, V_fm, kq_scr_tok[p][1])
            r.wait("vector", v_evac)
            o65a = r.op("vector", lambda v: v.memset(
                VTMa.rearrange("q (tt c) -> q tt c", c=65)[:, :, 64:65], 1.0))
            o65b = r.op("vector", lambda v: v.memset(
                VTMb.rearrange("q (tt c) -> q tt c", c=65)[:, :, 64:65], 1.0), "V")
            r.wait("scalar", v_evac)
            if prev_pair_scr is not None:
                r.wait("scalar", prev_pair_scr)
            tcopies = []
            for h01 in range(2):
                vtm = VTMa if h01 == 0 else VTMb
                for tt in range(32):
                    sl = TSTG[tt % 2]
                    idx = h01 * 32 + tt
                    if idx >= 2:
                        r.wait("scalar", tcopies[idx - 2])
                    ttok = r.dma("scalar", lambda s, h01=h01, tt=tt, sl=sl: s.dma_start_transpose(
                        out=sl, in_=V_fm[h01 * 64:(h01 + 1) * 64, tt * 128:(tt + 1) * 128]), "SD")
                    r.wait("vector", ttok)
                    if idx == 0:
                        r.wait("vector", o65b)
                    tcopies.append(r.op("vector", lambda v, vtm=vtm, tt=tt, sl=sl: v.tensor_copy(
                        out=vtm[:, tt * 65:tt * 65 + 64], in_=sl), "V"))
                head = 2 * p + h01
                r.wait("gpsimd", tcopies[-1])
                def mkv(head=head, vtm=vtm):
                    def f(g):
                        return g.dma_start(out=scr_V[head].transpose([1, 0, 2]),
                                           in_=vtm.rearrange("q (tt c) -> q tt c", c=65))
                    return f
                v_scr_tok[head] = r.dma("gpsimd", mkv(), "GD")
            prev_pair_scr = v_scr_tok[2 * p + 1]

        c_all_done = prev_pair_scr
        r.wait("gpsimd", c_dma_tok)
        cbias_tok = r.dma("gpsimd", lambda g: g.dma_start(
            out=CBIAS, in_=scr_c[0:1, 0:64].partition_broadcast(128)), "GD")
        if "cneg" in taps:
            r.wait("gpsimd", cbias_tok)
            tb = r.dma("gpsimd", lambda g: g.dma_start(out=tap_d["cneg"], in_=scr_c), "GD")
            tap_barrier(tb)

        KB = [bvv(0, 4096), bvv(2048, 4096)]
        QB = [bvv(4096, 2048), bvv(5120, 2048)]
        VA = [bvv(6144, 2080), bvv(8224, 2080)]
        VB = [bvv(7184, 2080), bvv(9264, 2080)]
        Pb = [bvv(10304, 2048), bvv(11328, 2048)]
        PVS_A = fv(12352, 2048, 0, 64)
        PVS_B = fv(32768, 2048, 0, 64)
        RLBC_A = fv(34816, 2048, 0, 64)
        RLBC_B = fv(36864, 2048, 0, 64)
        LR_A = fv(38912, 2048, 0, 1)
        LRr_A = fv(38912, 2048, 32, 33)
        LR_B = fv(38912, 2048, 64, 65)
        LRr_B = fv(38912, 2048, 96, 97)
        BST = bvv(40960, 2048)
        ATT = [bvv(24576 + pp * 1024, 2048) for pp in range(NPAIR)]

        def load_pair(p, parity, buf_free_tok):
            r.wait("sync", v_scr_tok[2 * p + 1])
            r.wait("sync", c_all_done)
            if buf_free_tok is not None:
                r.wait("sync", buf_free_tok)
            return [
                r.dma("sync", lambda s, p=p, parity=parity: s.dma_start(out=KB[parity], in_=scr_K[p])),
                r.dma("sync", lambda s, p=p, parity=parity: s.dma_start(out=QB[parity], in_=scr_Q[p])),
                r.dma("sync", lambda s, p=p, parity=parity: s.dma_start(
                    out=VA[parity].rearrange("q (tt c) -> q tt c", c=65),
                    in_=scr_V[2 * p].transpose([1, 0, 2]))),
                r.dma("sync", lambda s, p=p, parity=parity: s.dma_start(
                    out=VB[parity].rearrange("q (tt c) -> q tt c", c=65),
                    in_=scr_V[2 * p + 1].transpose([1, 0, 2]))),
            ]

        r.wait("tensor", cbias_tok)
        pair_load_toks = load_pair(0, 0, None)
        pair_last_T = {}
        pb_free = [None, None]
        sc_free = [None, None]
        last_exp = None
        bank_free = {0: None, 1: None}
        prev_att_mults = None
        att_done_tok = None

        for p in range(NPAIR):
            parity = p % 2
            if p + 1 < NPAIR:
                next_load = load_pair(p + 1, 1 - parity, pair_last_T.get(p - 1))
            for tok in pair_load_toks:
                r.wait("tensor", tok)
            evac_v_tok = None
            for qc in range(4):
                PVq = [PV0, PV1] if qc % 2 == 0 else [PSX, PSY]
                if bank_free[qc % 2] is not None:
                    r.wait("tensor", bank_free[qc % 2])
                for h01 in range(2):
                    PVps = PVq[h01]
                    ci = (2 * p + h01) * 4 + qc
                    pv_tok = None
                    exp_toks = [None] * 17
                    for u in range(17):
                        half = u % 2
                        if u < 16:
                            r.wait("tensor", sc_free[half])
                            sc_tok = None
                            for kt in range(2):
                                ki = u * 2 + kt
                                sc_tok = r.op("tensor", lambda t, h01=h01, kt=kt, ki=ki, qc=qc, parity=parity, hbase=half * 1024: t.matmul(
                                    out=PS_BIG[:, hbase + kt * 512:hbase + (kt + 1) * 512],
                                    lhsT=KB[parity][h01 * 64:(h01 + 1) * 64, ki * 128:(ki + 1) * 128],
                                    rhs=QB[parity][h01 * 64:(h01 + 1) * 64, qc * 512:(qc + 1) * 512],
                                    start=True, stop=True, tile_position=(h01 * 64, 0)),
                                    "T" if kt == 1 else None)
                            r.wait("scalar", sc_tok)
                            if pb_free[half] is not None:
                                r.wait("scalar", pb_free[half])
                            exp_toks[u] = r.op("scalar", lambda s, slot=half, ci=ci, hbase=half * 1024: s.activation(
                                out=Pb[slot][:, 0:1024], in_=PS_BIG[:, hbase:hbase + 1024], func=AF.Exp,
                                bias=CBIAS[:, ci:ci + 1], scale=1.0), "S")
                            sc_free[half] = exp_toks[u]
                        if u >= 1:
                            up = u - 1
                            slot = up % 2
                            r.wait("tensor", exp_toks[up])
                            for kt in range(2):
                                ki = up * 2 + kt
                                vbuf = VA[parity] if h01 == 0 else VB[parity]
                                pv_tok = r.op("tensor", lambda t, PVps=PVps, vbuf=vbuf, ki=ki, kt=kt, slot=slot: t.matmul(
                                    out=PVps[0:65, :],
                                    lhsT=vbuf[:, ki * 65:ki * 65 + 65],
                                    rhs=Pb[slot][:, kt * 512:(kt + 1) * 512],
                                    start=(ki == 0), stop=(ki == 31)),
                                    "T" if kt == 1 else None)
                            pb_free[slot] = pv_tok
                    r.wait("vector", pv_tok)
                    PVS = PVS_A if h01 == 0 else PVS_B
                    LR = LR_A if h01 == 0 else LR_B
                    if prev_att_mults is not None and qc == 0 and h01 == 0:
                        r.wait("vector", prev_att_mults)
                    r.op("vector", lambda v, PVS=PVS, qc=qc, PVps=PVps: v.tensor_copy(
                        out=PVS[0:64, qc * 512:(qc + 1) * 512], in_=PVps[0:64, :]))
                    evac_v_tok = r.op("vector", lambda v, LR=LR, qc=qc, PVps=PVps: v.tensor_copy(
                        out=LR[:, qc * 512:(qc + 1) * 512], in_=PVps[64:65, :]), "V")
                bank_free[qc % 2] = evac_v_tok
            r.op("vector", lambda v: v.reciprocal(out=LRr_A, in_=LR_A))
            rl_tok = r.op("vector", lambda v: v.reciprocal(out=LRr_B, in_=LR_B), "V")
            r.wait("gpsimd", rl_tok)
            r.dma("gpsimd", lambda g: g.dma_start(out=scr_rows[7:8, 0:NQ], in_=LRr_A), "GD")
            w2tk = r.dma("gpsimd", lambda g: g.dma_start(out=scr_rows[8:9, 0:NQ], in_=LRr_B), "GD")
            r.wait("gpsimd", w2tk)
            r.dma("gpsimd", lambda g: g.dma_start(out=RLBC_A, in_=scr_rows[7:8, 0:NQ].partition_broadcast(64)), "GD")
            rb_tok = r.dma("gpsimd", lambda g: g.dma_start(out=RLBC_B, in_=scr_rows[8:9, 0:NQ].partition_broadcast(64)), "GD")
            r.wait("vector", rb_tok)
            r.op("vector", lambda v, p=p: v.tensor_tensor(
                out=ATT[p][0:64, :], in0=PVS_A[0:64, :], in1=RLBC_A[0:64, :], op=ALU.mult))
            r.op("vector", lambda v: v.tensor_tensor(
                out=BST[0:64, :], in0=PVS_B[0:64, :], in1=RLBC_B[0:64, :], op=ALU.mult))
            prev_att_mults = r.op("vector", lambda v, p=p: v.tensor_copy(
                out=ATT[p][64:128, :], in_=BST[0:64, :]), "V")
            pair_last_T[p] = r.now("T")
            if p + 1 < NPAIR:
                pair_load_toks = next_load
        att_done_tok = prev_att_mults
        if "att" in taps:
            r.wait("vector", att_done_tok)
            tvv = r.op("vector", lambda v: v.tensor_copy(out=PVS_A[0:64, :], in_=ATT[0][0:64, :]))
            tvv = r.op("vector", lambda v: v.tensor_copy(out=fv(12352, 2048, 64, 128), in_=ATT[0][64:128, :]), "V")
            r.wait("gpsimd", tvv)
            tg = r.dma("gpsimd", lambda g: g.dma_start(out=tap_d["att"], in_=fv(12352, 2048)), "GD")
            tap_barrier(tg)
        if "pvs" in taps:
            r.wait("gpsimd", att_done_tok)
            tb = r.dma("gpsimd", lambda g: g.dma_start(out=tap_d["pvs"], in_=PVS_B[0:64, :]), "GD")
            tap_barrier(tb)
        if "lr" in taps:
            r.wait("gpsimd", att_done_tok)
            tb = r.dma("gpsimd", lambda g: g.dma_start(out=tap_d["lr"], in_=LR_B), "GD")
            tap_barrier(tb)

        H = [fv(ht * 512, 512) for ht in range(32)]
        HQ = [bvv(ht * 512, 512) for ht in range(32)]
        XF1bc = fv(44032, 2048)
        QFbc = fv(46080, 512)
        HF2bc = fv(46592, 512)
        W1t = [bvv(16384 - 0, 0), bvv(0, 0)]
        hs_col = sv(SCOL, 16)
        qf_col = sv(SCOL + 64, 16)
        hf2_col = sv(SCOL + 128, 16)

        r.wait("gpsimd", att_done_tok)
        xf1bc_tok = r.dma("gpsimd", lambda g: g.dma_start(
            out=XF1bc, in_=scr_rows[5:6, 0:NQ].partition_broadcast(128)), "GD")
        r.wait("sync", att_done_tok)
        r.wait("sync", tern_done)

        W1t = [SM_BIG_B[:, 0:1024], SM_BIG_B[:, 1024:2048]]
        W2t = [SM_BIG_B[:, 2048:3072], SM_BIG_B[:, 3072:4096]]
        DQ = [SM_BIG[:, 2048:2560], SM_BIG[:, 2560:3072]]
        Mh = SM_BIG[:, 3072:3584]
        TR1 = SM_BIG[0:64, 3584:4096]
        DQ2 = [SM_BIG[:, 3072:3584], SM_BIG[:, 3584:4096]]
        HQS = SM_BIG_B[:, 4096:4608]
        TR2 = SM_BIG[0:32, 2048:2560]

        gelu_slot = [None, None]
        deq_slot = [None, None]
        mm1_slot = [None, None]
        hq_prev_tc = None
        mm2_last_tc = None
        g_done = None
        for tc in range(4):
            csl = slice(tc * 512, (tc + 1) * 512)
            r.wait("vector", mm2_last_tc)
            r.wait("vector", g_done)
            mh_tok = r.op("vector", lambda v: v.memset(Mh, 0.0), "V")
            for ht in range(32):
                wslot = ht % 2
                wb = W1t[wslot]
                if mm1_slot[wslot] is not None:
                    r.wait("sync", mm1_slot[wslot])
                dtk = r.dma("sync", lambda s, ht=ht, wb=wb: s.dma_start(out=wb, in_=scr_W1q[ht]))
                r.wait("tensor", dtk)
                if tc == 0 and ht == 0:
                    r.wait("tensor", xq_done)
                ps = PS_BIG[:, 0:512] if ht % 2 == 0 else PS_BIG[:, 1024:1536]
                if deq_slot[wslot] is not None:
                    r.wait("tensor", deq_slot[wslot])
                mmtok = None
                for dc in range(8):
                    mmtok = r.op("tensor", lambda t, dc=dc, wb=wb, ps=ps, csl=csl: t.matmul(
                        out=ps, lhsT=wb[:, dc * 128:(dc + 1) * 128],
                        rhs=XQ[dc][:, csl], start=(dc == 0), stop=(dc == 7)),
                        "T" if dc == 7 else None)
                mm1_slot[wslot] = mmtok
                r.wait("vector", mmtok)
                if tc == 0 and ht == 0:
                    r.wait("vector", xf1bc_tok)
                dq = DQ[wslot]
                if gelu_slot[wslot] is not None:
                    r.wait("vector", gelu_slot[wslot])
                if hq_prev_tc is not None and ht < 2:
                    r.wait("scalar", hq_prev_tc)
                if mm2_last_tc is not None and ht < 2:
                    r.wait("scalar", mm2_last_tc)
                deq_slot[wslot] = r.op("vector", lambda v, dq=dq, ps=ps, csl=csl: v.tensor_tensor(
                    out=dq, in0=ps, in1=XF1bc[:, csl], op=ALU.mult), "V")
                r.wait("scalar", deq_slot[wslot])
                gelu_slot[wslot] = r.op("scalar", lambda s, ht=ht, dq=dq: s.activation(
                    out=H[ht], in_=dq, func=AF.Gelu, bias=B1C[:, ht:ht + 1], scale=1.0), "S")
                r.wait("vector", gelu_slot[wslot])
                r.op("vector", lambda v, ht=ht: v.scalar_tensor_tensor(
                    out=Mh, in0=H[ht], scalar=-1.0, in1=Mh, op0=ALU.mult, op1=ALU.max))
                r.op("vector", lambda v, ht=ht: v.tensor_max(out=Mh, in0=Mh, in1=H[ht]))
            r.op("vector", lambda v: v.tensor_copy(out=TR1[0:64, :], in_=Mh[64:128, :]))
            r.op("vector", lambda v: v.tensor_tensor(out=Mh[0:64, :], in0=Mh[0:64, :], in1=TR1[0:64, :], op=ALU.max))
            r.op("vector", lambda v: v.tensor_copy(out=TR2[0:32, :], in_=Mh[32:64, :]))
            r.op("vector", lambda v: v.tensor_tensor(out=Mh[0:32, :], in0=Mh[0:32, :], in1=TR2[0:32, :], op=ALU.max))
            r.op("vector", lambda v: v.transpose(out=TR2[0:32, :], in_=Mh[0:32, :]))
            vop(lambda v: v.tensor_reduce(
                out=XS2[0:32, 0:16], in_=TR2[0:32, :].rearrange("p (b f) -> p b f", f=32),
                axis=AX.X, op=ALU.max))
            vop(lambda v: v.tensor_scalar_max(out=hs_col[0:32, :], in0=XS2[0:32, 0:16], scalar1=EPS))
            vop(lambda v: v.reciprocal(out=XS3[0:32, 0:16], in_=hs_col[0:32, :]))
            vop(lambda v: v.tensor_scalar_mul(out=qf_col[0:32, :], in0=XS3[0:32, 0:16], scalar1=127.0))
            hf_tok = vop(lambda v: v.tensor_scalar(
                out=hf2_col[0:32, :], in0=hs_col[0:32, :],
                scalar1=WSC[0:32, 2:3], scalar2=1.0 / 127.0, op0=ALU.mult, op1=ALU.mult))
            r.wait("gpsimd", hf_tok)
            for slot, col in ((9, qf_col), (10, hf2_col)):
                colw = sv(SCOL + 64, 32) if col is qf_col else sv(SCOL + 128, 32)
                vt_tok = vop(lambda v, colw=colw: v.transpose(
                    out=TCOL[0:32, 96:128], in_=colw[0:32, :]))
                r.wait("gpsimd", vt_tok)
                def mk3(slot=slot, tc=tc):
                    def f(g):
                        dst = scr_rows[slot:slot + 1, tc * 512:(tc + 1) * 512].rearrange(
                            "o (b p) -> o b p", p=32).squeeze(0)
                        return g.dma_start(out=dst, in_=TCOL[0:16, 96:128])
                    return f
                stk = r.dma("gpsimd", mk3(), "GD")
                r.wait("vector", stk)
            hrow_tok = r.now("GD")
            r.wait("gpsimd", hrow_tok)
            r.dma("gpsimd", lambda g, tc=tc: g.dma_start(
                out=QFbc, in_=scr_rows[9:10, tc * 512:(tc + 1) * 512].partition_broadcast(128)), "GD")
            hbc_tok = r.dma("gpsimd", lambda g, tc=tc: g.dma_start(
                out=HF2bc, in_=scr_rows[10:11, tc * 512:(tc + 1) * 512].partition_broadcast(128)), "GD")
            r.wait("vector", hbc_tok)
            hq_last = None
            for ht in range(32):
                r.op("vector", lambda v, ht=ht: v.tensor_mul(out=H[ht], in0=H[ht], in1=QFbc))
                r.op("vector", lambda v, ht=ht: v.tensor_scalar(
                    out=HQS, in0=H[ht], scalar1=MAGIC, scalar2=MAGIC,
                    op0=ALU.add, op1=ALU.subtract))
                hq_last = r.op("vector", lambda v, ht=ht: v.tensor_copy(
                    out=HQ[ht], in_=HQS), "V" if ht == 31 else None)
            hq_prev_tc = hq_last
            if "h0" in taps and tc == 0:
                r.wait("gpsimd", hq_last)
                tb = r.dma("gpsimd", lambda g: g.dma_start(out=tap_d["h0"], in_=H[0]), "GD")
                tap_barrier(tb)
            mm2_tok = None
            sil_slot = [None, None]
            deq2_slot = [None, None]
            w2_dma_slot = [None, None]
            for dt in range(8):
                psl = dt % 2
                ps = PSX[:, 0:512] if psl == 0 else PSY[:, 0:512]
                if deq2_slot[psl] is not None:
                    r.wait("tensor", deq2_slot[psl])
                for grp in range(4):
                    gslot = grp % 2
                    wb = W2t[gslot]
                    if w2_dma_slot[gslot] is not None:
                        r.wait("sync", w2_dma_slot[gslot])
                    d2 = r.dma("sync", lambda s, dt=dt, grp=grp, wb=wb: s.dma_start(
                        out=wb, in_=scr_W2q[dt, grp]))
                    r.wait("tensor", d2)
                    if dt == 0 and grp == 0:
                        r.wait("tensor", hq_last)
                    for j in range(8):
                        hc = grp * 8 + j
                        mm2_tok = r.op("tensor", lambda t, j=j, hc=hc, wb=wb, ps=ps: t.matmul(
                            out=ps, lhsT=wb[:, j * 128:(j + 1) * 128],
                            rhs=HQ[hc], start=(hc == 0), stop=(hc == 31)),
                            "T" if j == 7 else None)
                    w2_dma_slot[gslot] = mm2_tok
                r.wait("vector", mm2_tok)
                dq = DQ2[psl]
                if sil_slot[psl] is not None:
                    r.wait("vector", sil_slot[psl])
                deq2_slot[psl] = r.op("vector", lambda v, dq=dq, ps=ps: v.tensor_tensor(
                    out=dq, in0=ps, in1=HF2bc, op=ALU.mult), "V")
                r.wait("scalar", deq2_slot[psl])
                sil_slot[psl] = r.op("scalar", lambda s, dt=dt, dq=dq, csl=csl: s.activation(
                    out=bvv(32768 + dt * 1024, 2048)[:, csl], in_=dq,
                    func=AF.Silu, bias=B2C[:, dt:dt + 1], scale=1.0), "S")
            mm2_last_tc = r.now("T")
            g_done = sil_slot[(8 - 1) % 2]
        G = [bvv(32768 + dt * 1024, 2048) for dt in range(8)]

        s_chunk = fv(0, 2048)
        xn_chunk = fv(2048, 2048)
        o_chunk = fv(4096, 2048)
        xF = fv(6144, 2048)
        wo_stage = fv(8192, 1024)
        WOB = [bvv(9216 + ac * 512, 1024) for ac in range(8)]
        A1b2 = fv(13312, 2048)
        B1b2 = fv(15360, 2048)

        r.wait("gpsimd", g_done)
        r.dma("gpsimd", lambda g: g.dma_start(out=A1b2, in_=scr_rows[2:3, 0:NQ].partition_broadcast(128)), "GD")
        fbc_tok = r.dma("gpsimd", lambda g: g.dma_start(out=B1b2, in_=scr_rows[3:4, 0:NQ].partition_broadcast(128)), "GD")

        r.wait("sync", g_done)
        wo_cast = None
        for ac in range(8):
            if wo_cast is not None:
                r.wait("sync", wo_cast)
            dtk = r.dma("sync", lambda s, ac=ac: s.dma_start(
                out=wo_stage, in_=woT[ac * 128:(ac + 1) * 128, :]))
            r.wait("vector", dtk)
            wo_cast = r.op("vector", lambda v, ac=ac: v.tensor_copy(out=WOB[ac], in_=wo_stage), "V")

        st_count = 0
        prev_out_v = None
        prev_evacs = []
        last_psum_user_v = r.now("V")
        for dt in range(8):
            if prev_out_v is not None:
                r.wait("sync", prev_out_v)
            xtk = r.dma("sync", lambda s, dt=dt: s.dma_start(
                out=xF, in_=x_t[dt * 128:(dt + 1) * 128, 0:NQ]))
            evac_toks = []
            for qc in range(4):
                ps = PSX if qc % 2 == 0 else PSY
                if dt == 0 and qc == 0:
                    r.wait("tensor", wo_cast)
                    r.wait("tensor", last_psum_user_v)
                idx = dt * 4 + qc
                allev = prev_evacs + evac_toks
                if idx >= 2:
                    r.wait("tensor", allev[idx - 2])
                wo_mm = None
                for ac in range(8):
                    wo_mm = r.op("tensor", lambda t, ac=ac, qc=qc, dt=dt, ps=ps: t.matmul(
                        out=ps[:, 0:512], lhsT=WOB[ac][:, dt * 128:(dt + 1) * 128],
                        rhs=ATT[ac][:, qc * 512:(qc + 1) * 512],
                        start=(ac == 0), stop=(ac == 7)), "T" if ac == 7 else None)
                r.wait("scalar", wo_mm)
                if dt > 0 and qc == 0:
                    r.wait("scalar", prev_out_v)
                ev = r.op("scalar", lambda s, qc=qc, dt=dt, ps=ps: s.activation(
                    out=s_chunk[:, qc * 512:(qc + 1) * 512], in_=ps[:, 0:512],
                    func=AF.Identity, bias=BOC[:, dt:dt + 1], scale=1.0), "S")
                evac_toks.append(ev)
            prev_evacs = prev_evacs + evac_toks
            r.wait("vector", evac_toks[-1])
            r.wait("vector", xtk)
            if dt == 0:
                r.wait("vector", fbc_tok)
            if prev_out_v is not None:
                r.wait("vector", ("ST", st_count))
            r.op("vector", lambda v: v.tensor_mul(out=xn_chunk, in0=xF, in1=A1b2))
            r.op("vector", lambda v: v.tensor_add(out=xn_chunk, in0=xn_chunk, in1=B1b2))
            r.op("vector", lambda v: v.tensor_add(out=s_chunk, in0=s_chunk, in1=xn_chunk))
            r.op("vector", lambda v, dt=dt: v.tensor_tensor(out=o_chunk, in0=s_chunk, in1=G[dt], op=ALU.mult))
            prev_out_v = r.op("vector", lambda v: v.tensor_add(out=o_chunk, in0=o_chunk, in1=xF), "V")
            r.wait("sync", prev_out_v)
            r.dma("sync", lambda s, dt=dt: s.dma_start(
                out=out_d[dt * 128:(dt + 1) * 128, :], in_=o_chunk), "ST")
            st_count += 16
        r.wait("sync", ("ST", st_count))

        for name in ("sync", "gpsimd", "tensor", "vector", "scalar"):
            items = r.q[name]
            getattr(block, name)(lambda eng, items=items: _replay(eng, items, sems))

    return nc


_NC_CACHE = {}


def _get_nc(taps=()):
    key = tuple(sorted(taps))
    if key not in _NC_CACHE:
        _NC_CACHE[key] = build(taps)
    return _NC_CACHE[key]


def _rope_tables():
    i = np.arange(0, 64, 2, dtype=np.float32) / np.float32(64.0)
    inv_freq = (np.float32(1.0) / (np.float32(10000.0) ** i)).astype(np.float32)
    t = np.arange(SEQ, dtype=np.float32)
    ang = np.outer(t, inv_freq).astype(np.float32)
    cos = np.cos(ang).astype(np.float32).T
    sin = np.sin(ang).astype(np.float32).T
    C = np.tile(cos, (4, 1))
    S = np.concatenate([-sin, sin, -sin, sin], axis=0)
    return C, S


def make_in_maps(x, ln_w, ln_b, Wq, bq, Wk, bk, Wv, bv, Wo, bo, W1, b1, W2, b2):
    perm = np.concatenate([h * 64 + np.concatenate([np.arange(0, 64, 2), np.arange(1, 64, 2)])
                           for h in range(NHEAD)]).astype(np.int64)
    asc = np.ascontiguousarray
    f32 = np.float32
    bf = ml_dtypes.bfloat16
    wqT_ = asc(np.asarray(Wq, f32).T[:, perm])
    wkT_ = asc(np.asarray(Wk, f32).T[:, perm])
    wvT_ = asc(np.asarray(Wv, f32).T)
    woT_ = asc(np.asarray(Wo, f32).T)
    col8 = lambda a: asc(np.asarray(a, f32).reshape(8, 128).T)
    col32 = lambda a: asc(np.asarray(a, f32).reshape(32, 128).T)
    shared = dict(
        wqT=wqT_, wkT=wkT_, wvT=wvT_, woT=woT_,
        bqc=col8(np.asarray(bq, f32)[perm]), bkc=col8(np.asarray(bk, f32)[perm]),
        bvc=col8(bv), boc=col8(bo), lnwc=col8(ln_w), lnbc=col8(ln_b),
        b1c=col32(b1), b2c=col8(b2),
        w1T=asc(np.asarray(W1, f32).T), w2T=asc(np.asarray(W2, f32).T),
    )
    C, S = _rope_tables()
    in_maps = []
    for core in range(8):
        b = core // 2
        qlo = (core % 2) * NQ
        olo = NQ - qlo
        order = np.concatenate([np.arange(qlo, qlo + NQ), np.arange(olo, olo + NQ)])
        xT = np.asarray(x[b], f32).T
        m = dict(shared)
        m["x_t"] = asc(xT[:, order])
        m["ropeCK"] = asc(C[:, order]).astype(bf)
        m["ropeSK"] = asc(S[:, order]).astype(bf)
        m["ropeCQ"] = (asc(C[:, qlo:qlo + NQ]) / np.float32(8.0)).astype(bf)
        m["ropeSQ"] = (asc(S[:, qlo:qlo + NQ]) / np.float32(8.0)).astype(bf)
        in_maps.append(m)
    return in_maps


def kernel(L0_attn_x, L0_attn_ln_w, L0_attn_ln_b, L0_attn_Wq, L0_attn_bq,
           L0_attn_Wk, L0_attn_bk, L0_attn_Wv, L0_attn_bv, L0_attn_Wo,
           L0_attn_bo, L0_ffn_W1, L0_ffn_b1, L0_ffn_W2, L0_ffn_b2):
    in_maps = make_in_maps(L0_attn_x, L0_attn_ln_w, L0_attn_ln_b, L0_attn_Wq,
                           L0_attn_bq, L0_attn_Wk, L0_attn_bk, L0_attn_Wv,
                           L0_attn_bv, L0_attn_Wo, L0_attn_bo, L0_ffn_W1,
                           L0_ffn_b1, L0_ffn_W2, L0_ffn_b2)
    nc = _get_nc()
    res = run_bass_kernel_spmd(nc, in_maps, core_ids=list(range(8)))
    out = np.empty((4, SEQ, D), dtype=np.float32)
    for core in range(8):
        b = core // 2
        qlo = (core % 2) * NQ
        out[b, qlo:qlo + NQ, :] = np.asarray(res.results[core]["out"], dtype=np.float32).T
    return out


# version 12
# speedup vs baseline: 1.1974x; 1.0404x over previous
import numpy as np
import ml_dtypes
import concourse.bass as bass
import concourse.mybir as mybir
from concourse.bass_utils import run_bass_kernel_spmd
from contextlib import ExitStack

F32 = mybir.dt.float32
BF16 = mybir.dt.bfloat16
AF = mybir.ActivationFunctionType
ALU = mybir.AluOpType
AX = mybir.AxisListType

D = 1024
SEQ = 4096
NQ = 2048
HFF = 4096
NHEAD = 16
NPAIR = 8
MAGIC = 12582912.0
EPS = 1e-5

SEMS = ("LD", "GD", "SD", "T", "V", "S", "ST")


class Rec:
    def __init__(self):
        self.q = {k: [] for k in ("sync", "gpsimd", "tensor", "vector", "scalar")}
        self.c = {k: 0 for k in SEMS}

    def op(self, eng, fn, inc=None, k=1):
        self.q[eng].append(("op", fn, inc, k))
        if inc is not None:
            self.c[inc] += k
            return (inc, self.c[inc])
        return None

    def dma(self, eng, fn, inc="LD"):
        return self.op(eng, fn, inc, 16)

    def wait(self, eng, tok):
        if tok is not None:
            self.q[eng].append(("wait", tok[0], tok[1]))

    def now(self, sem):
        return (sem, self.c[sem])


def _replay(engine, items, sems):
    for it in items:
        if it[0] == "wait":
            engine.wait_ge(sems[it[1]], it[2])
        else:
            ins = it[1](engine)
            if it[2] is not None:
                ins.then_inc(sems[it[2]], it[3])


AR_N = 47616
SM_N = 896


def build(taps=()):
    taps = set(taps)
    nc = bass.Bass("TRN2", target_bir_lowering=False, debug=False)

    def din(name, shape, dt=F32):
        return nc.dram_tensor(name, shape, dt, kind="ExternalInput").ap()

    x_t = din("x_t", [D, SEQ])
    wqT = din("wqT", [D, D]); wkT = din("wkT", [D, D])
    wvT = din("wvT", [D, D]); woT = din("woT", [D, D])
    bqc = din("bqc", [128, 8]); bkc = din("bkc", [128, 8])
    bvc = din("bvc", [128, 8]); boc = din("boc", [128, 8])
    lnwc = din("lnwc", [128, 8]); lnbc = din("lnbc", [128, 8])
    b1c = din("b1c", [128, 32]); b2c = din("b2c", [128, 8])
    w1T = din("w1T", [D, HFF]); w2T = din("w2T", [HFF, D])
    ropeCK = din("ropeCK", [128, SEQ], BF16); ropeSK = din("ropeSK", [128, SEQ], BF16)
    ropeCQ = din("ropeCQ", [128, NQ], BF16); ropeSQ = din("ropeSQ", [128, NQ], BF16)

    out_d = nc.dram_tensor("out", [D, NQ], F32, kind="ExternalOutput").ap()

    def dscr(name, shape, dt):
        return nc.dram_tensor(name, shape, dt, kind="Internal").ap()

    scr_rows = dscr("scr_rows", [16, SEQ], F32)
    scr_c = dscr("scr_c", [1, 256], F32)
    scr_K = dscr("scr_K", [NPAIR, 128, SEQ], BF16)
    scr_Q = dscr("scr_Q", [NPAIR, 128, NQ], BF16)
    scr_V = dscr("scr_V", [16, 32, 128, 65], BF16)
    scr_XQ = dscr("scr_XQ", [8, 128, NQ], BF16)
    scr_W1q = dscr("scr_W1q", [32, 128, 1024], BF16)
    scr_W2q = dscr("scr_W2q", [8, 4, 128, 1024], BF16)

    TAPDEFS = {
        "stats": ([128, 256], F32), "xn2": ([128, SEQ], F32), "xq": ([128, NQ], F32),
        "cneg": ([1, 256], F32), "att": ([128, NQ], F32), "g0": ([128, NQ], F32),
        "w1q": ([128, HFF], F32), "h0": ([128, 512], F32), "pvs": ([64, NQ], F32),
        "wsc": ([128, 64], F32),
        "lr": ([1, NQ], F32),
    }
    tap_d = {t: nc.dram_tensor("tap_" + t, TAPDEFS[t][0], TAPDEFS[t][1],
                               kind="ExternalOutput").ap() for t in taps}

    r = Rec()

    with ExitStack() as ctx:
        e = ctx.enter_context
        AR = e(nc.sbuf_tensor("AR", [128, AR_N], F32))
        SM = e(nc.sbuf_tensor("SM", [128, SM_N], F32))
        SM_BIG = e(nc.sbuf_tensor("SM_BIG", [128, 4096], F32))
        SM_BIG_B = SM_BIG.ap().bitcast(BF16)
        AR_B = AR.ap().bitcast(BF16)
        SM_B = SM.ap().bitcast(BF16)

        def fv(off, n, p0=0, p1=128):
            return AR[p0:p1, off:off + n]

        def bvv(off, nb, p0=0, p1=128):
            return AR_B[p0:p1, 2 * off:2 * off + nb]

        def sv(off, n, p0=0, p1=128):
            return SM[p0:p1, off:off + n]

        ones_bf = SM_B[:, 0:1]
        ones_f = sv(1, 1)
        SCOL = 8
        def scol(i, n=32):
            return sv(SCOL + 32 * i, n)
        W1S = sv(300, 8); W2S = sv(308, 32)
        WSC = sv(340, 4)
        BQC = sv(360, 8); BKC = sv(368, 8); BVC = sv(376, 8); BOC = sv(384, 8)
        LNW = sv(392, 8); LNB = sv(400, 8)
        B1C = sv(408, 32); B2C = sv(440, 8)
        CBIAS = sv(448, 64)
        NEGC = sv(512, 32, 0, 32)
        NEGCT = sv(544, 32, 0, 32)
        CFIN = sv(576, 1, 0, 32)
        WS11 = sv(578, 1, 0, 1); WS12 = sv(579, 1, 0, 1)
        WS21 = sv(580, 1, 0, 1); WS22 = sv(581, 1, 0, 1)
        W1T_ = sv(584, 1); W2T_ = sv(585, 1)
        EPSC = sv(586, 1)
        TCOL = sv(608, 128)
        XS2 = sv(736, 64)
        XS3 = sv(800, 64)
        CF2 = sv(592, 1, 0, 32)

        PS_BIG = e(nc.psum_tensor("PS_BIG", [128, 2048], F32))
        PV0 = e(nc.psum_tensor("PV0", [128, 512], F32))
        PV1 = e(nc.psum_tensor("PV1", [128, 512], F32))
        PSX = e(nc.psum_tensor("PSX", [128, 512], F32))
        PSY = e(nc.psum_tensor("PSY", [128, 512], F32))

        sems = {k: e(nc.semaphore(k)) for k in SEMS}
        block = e(nc.Block())

        def tap_barrier(tok):
            for _eng in ("sync", "vector", "scalar", "tensor", "gpsimd"):
                r.wait(_eng, tok)

        VOPT = 1

        def vop(fn):
            if VOPT == 0:
                return r.op("vector", fn, "V")
            if VOPT == 2:
                tok = r.op("vector", fn, "V")
                r.op("vector", lambda v: v.engine_nop())
                r.op("vector", lambda v: v.engine_nop())
                return tok
            tok = r.op("vector", fn, "V")
            r.wait("vector", tok)
            return tok

        def sop(fn):
            if VOPT == 0:
                return r.op("scalar", fn, "S")
            if VOPT == 2:
                tok = r.op("scalar", fn, "S")
                r.op("scalar", lambda x: x.engine_nop())
                r.op("scalar", lambda x: x.engine_nop())
                return tok
            tok = r.op("scalar", fn, "S")
            r.wait("scalar", tok)
            return tok


        xchunk = [fv(0, 4096), fv(4096, 4096)]
        xbuf = bvv(8192, 4096)
        sqbuf = bvv(10240, 4096)
        wchunk = [fv(12288, 4096), fv(16384, 4096)]

        r.op("vector", lambda v: v.memset(ones_bf, 1.0))
        r.op("vector", lambda v: v.memset(ones_f, 1.0))
        r.op("vector", lambda v: v.memset(EPSC, EPS), "V")
        for dst, src in ((BQC, bqc), (BKC, bkc), (BVC, bvc), (BOC, boc),
                         (LNW, lnwc), (LNB, lnbc), (B1C, b1c), (B2C, b2c)):
            r.dma("gpsimd", lambda g, dst=dst, src=src: g.dma_start(out=dst, in_=src), "GD")

        cast_toks = [None] * 8
        stats_mm_tok = None
        r.wait("tensor", r.now("V"))
        r.op("tensor", lambda t: t.matmul(out=PSX[:, 500:501], lhsT=SM_B[:, 0:128], rhs=ones_bf, start=True, stop=True))
        r.op("tensor", lambda t: t.matmul(out=PSY[:, 500:501], lhsT=SM_B[:, 0:128], rhs=ones_bf, start=True, stop=True))
        wred_toks = [None] * 40
        for dc in range(8):
            b = dc % 2
            if dc >= 2:
                r.wait("sync", cast_toks[dc - 2])
            xd = r.dma("sync", lambda s, dc=dc, b=b: s.dma_start(
                out=xchunk[b], in_=x_t[dc * 128:(dc + 1) * 128, :]))
            r.wait("vector", xd)
            if stats_mm_tok is not None:
                r.wait("vector", stats_mm_tok)
            r.op("vector", lambda v, b=b: v.tensor_copy(out=xbuf, in_=xchunk[b]))
            cast_toks[dc] = r.op("vector", lambda v: v.tensor_tensor(
                out=sqbuf, in0=xbuf, in1=xbuf, op=ALU.mult), "V")
            r.wait("tensor", cast_toks[dc])
            for tt in range(32):
                r.op("tensor", lambda t, tt=tt, dc=dc: t.matmul(
                    out=PSX[:, tt:tt + 1], lhsT=xbuf[:, tt * 128:(tt + 1) * 128],
                    rhs=ones_bf, start=False, stop=True))
            for tt in range(32):
                stats_mm_tok = r.op("tensor", lambda t, tt=tt, dc=dc: t.matmul(
                    out=PSY[:, tt:tt + 1], lhsT=sqbuf[:, tt * 128:(tt + 1) * 128],
                    rhs=ones_bf, start=False, stop=True),
                    "T" if tt == 31 else None)
        stats_done = stats_mm_tok

        for i in range(40):
            b = i % 2
            if i >= 2:
                r.wait("gpsimd", wred_toks[i - 2])
            if i < 8:
                wd = r.dma("gpsimd", lambda g, i=i, b=b: g.dma_start(
                    out=wchunk[b], in_=w1T[i * 128:(i + 1) * 128, :]), "GD")
                fn = lambda v, i=i, b=b: v.tensor_reduce(
                    out=W1S[:, i:i + 1], in_=wchunk[b], axis=AX.X, op=ALU.add,
                    apply_absolute_value=True)
            else:
                wd = r.dma("gpsimd", lambda g, i=i, b=b: g.dma_start(
                    out=wchunk[b][:, 0:1024], in_=w2T[(i - 8) * 128:(i - 7) * 128, :]), "GD")
                fn = lambda v, i=i, b=b: v.tensor_reduce(
                    out=W2S[:, i - 8:i - 7], in_=wchunk[b][:, 0:1024], axis=AX.X,
                    op=ALU.add, apply_absolute_value=True)
            r.wait("vector", wd)
            wred_toks[i] = r.op("vector", fn, "V")
        wred_done = wred_toks[39]

        r.op("vector", lambda v: v.tensor_reduce(out=W1T_, in_=W1S, axis=AX.X, op=ALU.add))
        tot_tok = r.op("vector", lambda v: v.tensor_reduce(out=W2T_, in_=W2S, axis=AX.X, op=ALU.add), "V")
        r.wait("tensor", tot_tok)
        r.op("tensor", lambda t: t.matmul(out=PSX[0:1, 40:41], lhsT=W1T_, rhs=ones_f, start=False, stop=True))
        wtot_mm = r.op("tensor", lambda t: t.matmul(out=PSX[0:1, 41:42], lhsT=W2T_, rhs=ones_f, start=False, stop=True), "T")
        r.wait("vector", wtot_mm)
        vop(lambda v: v.tensor_scalar(out=WS11, in0=PSX[0:1, 40:41], scalar1=1.0 / (D * HFF), scalar2=EPS, op0=ALU.mult, op1=ALU.max))
        vop(lambda v: v.tensor_scalar(out=WS21, in0=PSX[0:1, 41:42], scalar1=1.0 / (D * HFF), scalar2=EPS, op0=ALU.mult, op1=ALU.max))
        vop(lambda v: v.reciprocal(out=WS12, in_=WS11))
        ws_tok = vop(lambda v: v.reciprocal(out=WS22, in_=WS21))
        r.wait("gpsimd", ws_tok)
        r.dma("gpsimd", lambda g: g.dma_start(out=scr_rows[6:7, 0:1], in_=WS11), "GD")
        r.dma("gpsimd", lambda g: g.dma_start(out=scr_rows[6:7, 1:2], in_=WS12), "GD")
        r.dma("gpsimd", lambda g: g.dma_start(out=scr_rows[6:7, 2:3], in_=WS21), "GD")
        t4 = r.dma("gpsimd", lambda g: g.dma_start(out=scr_rows[6:7, 3:4], in_=WS22), "GD")
        r.wait("gpsimd", t4)
        wsbc_tok = r.dma("gpsimd", lambda g: g.dma_start(
            out=WSC, in_=scr_rows[6:7, 0:4].partition_broadcast(128)), "GD")
        if "wsc" in taps:
            r.wait("gpsimd", wsbc_tok)
            tb = r.dma("gpsimd", lambda g: g.dma_start(out=tap_d["wsc"], in_=sv(300, 64)), "GD")
            tap_barrier(tb)

        r.wait("vector", stats_done)
        vop(lambda v: v.tensor_scalar_mul(out=scol(0), in0=PSX[:, 0:32], scalar1=1.0 / D))
        vop(lambda v: v.tensor_scalar_mul(out=scol(1), in0=PSY[:, 0:32], scalar1=1.0 / D))
        vop(lambda v: v.tensor_tensor(out=scol(7), in0=scol(0), in1=scol(0), op=ALU.mult))
        v_tok = vop(lambda v: v.tensor_sub(out=scol(1), in0=scol(1), in1=scol(7)))
        r.wait("scalar", v_tok)
        sop(lambda s: s.activation(out=scol(7), in_=scol(1), func=AF.Ln, bias=EPSC, scale=1.0))
        s_tok = sop(lambda s: s.activation(out=scol(2), in_=scol(7), func=AF.Exp, bias=0.0, scale=-0.5))
        r.wait("vector", s_tok)
        vop(lambda v: v.tensor_mul(out=scol(7), in0=scol(0), in1=scol(2)))
        vop(lambda v: v.tensor_scalar_mul(out=scol(3), in0=scol(7), scalar1=-1.0))
        vop(lambda v: v.tensor_mul(out=scol(7), in0=scol(2), in1=scol(2)))
        v2_tok = vop(lambda v: v.tensor_mul(out=scol(7), in0=scol(7), in1=scol(1)))
        r.wait("scalar", v2_tok)
        sop(lambda s: s.activation(out=XS2[:, 0:32], in_=scol(7), func=AF.Ln, bias=EPSC, scale=1.0))
        s2_tok = sop(lambda s: s.activation(out=scol(4), in_=XS2[:, 0:32], func=AF.Exp, bias=0.0, scale=-0.5))
        r.wait("vector", s2_tok)
        vop(lambda v: v.tensor_mul(out=scol(5), in0=scol(2), in1=scol(4)))
        ln_tok = vop(lambda v: v.tensor_mul(out=scol(6), in0=scol(3), in1=scol(4)))
        for slot, idx in ((0, 5), (1, 6), (2, 2), (3, 3)):
            vt_tok = None
            for bi in range(4):
                vt_tok = vop(lambda v, idx=idx, bi=bi: v.transpose(
                    out=TCOL[bi * 32:(bi + 1) * 32, 0:32],
                    in_=scol(idx)[bi * 32:(bi + 1) * 32, :]))
            r.wait("gpsimd", vt_tok)
            stk = None
            for bi in range(4):
                def mk(slot=slot, bi=bi):
                    def f(g):
                        dst = scr_rows[slot:slot + 1, :].rearrange(
                            "o (tt q p) -> o tt q p", q=4, p=32)[:, :, bi, :].squeeze(0)
                        return g.dma_start(out=dst, in_=TCOL[bi * 32:(bi + 1) * 32, 0:32])
                    return f
                stk = r.dma("gpsimd", mk(), "GD")
            r.wait("vector", stk)
        rows_tok = r.now("GD")
        if "stats" in taps:
            tb = r.dma("gpsimd", lambda g: g.dma_start(out=tap_d["stats"], in_=sv(SCOL, 256)), "GD")
            tap_barrier(tb)

        A2bc = fv(32768, 4096); B2bc = fv(36864, 4096)
        A1bc = fv(40960, 2048); B1bc = fv(43008, 2048)
        r.wait("gpsimd", rows_tok)
        r.dma("gpsimd", lambda g: g.dma_start(out=A2bc, in_=scr_rows[0:1, :].partition_broadcast(128)), "GD")
        r.dma("gpsimd", lambda g: g.dma_start(out=B2bc, in_=scr_rows[1:2, :].partition_broadcast(128)), "GD")
        r.dma("gpsimd", lambda g: g.dma_start(out=A1bc, in_=scr_rows[2:3, 0:NQ].partition_broadcast(128)), "GD")
        bc_tok = r.dma("gpsimd", lambda g: g.dma_start(out=B1bc, in_=scr_rows[3:4, 0:NQ].partition_broadcast(128)), "GD")

        tscr = fv(24576, 4096)
        tscr2 = fv(8192, 4096)
        tout2 = [bvv(28672, 4096), bvv(30720, 4096)]
        r.wait("sync", wsbc_tok)
        r.wait("sync", wred_done)
        r.wait("vector", stats_done)
        tern_vtoks = [None] * 40
        tern_wtok = None
        tern_wtoks = [None] * 40
        for i in range(40):
            b = i % 2
            if i < 8:
                src = w1T[i * 128:(i + 1) * 128, :]; ncols = 4096
                dc = i
                dst = scr_W1q.rearrange("h p (dcc f) -> h p dcc f", f=128)[:, :, dc, :].transpose([1, 0, 2])
                winv = WSC[:, 1:2]
            else:
                src = w2T[(i - 8) * 128:(i - 7) * 128, :]; ncols = 1024
                hc = i - 8
                dst = scr_W2q.rearrange("d g p (j f) -> d g p j f", f=128)[:, hc // 8, :, hc % 8, :].transpose([1, 0, 2])
                winv = WSC[:, 3:4]
            if i >= 2:
                r.wait("sync", tern_vtoks[i - 2])
            dtk = r.dma("sync", lambda s, src=src, b=b, ncols=ncols: s.dma_start(
                out=wchunk[b][:, 0:ncols], in_=src))
            r.wait("vector", dtk)
            if i >= 2 and tern_wtoks[i - 2] is not None:
                r.wait("vector", tern_wtoks[i - 2])
            r.op("vector", lambda v, b=b, ncols=ncols, winv=winv: v.tensor_scalar(
                out=tscr[:, 0:ncols], in0=wchunk[b][:, 0:ncols],
                scalar1=winv, scalar2=MAGIC, op0=ALU.mult, op1=ALU.add))
            r.op("vector", lambda v, ncols=ncols: v.tensor_scalar(
                out=tscr2[:, 0:ncols], in0=tscr[:, 0:ncols],
                scalar1=MAGIC, scalar2=1.0, op0=ALU.subtract, op1=ALU.min))
            tern_vtoks[i] = r.op("vector", lambda v, ncols=ncols, b=b: v.tensor_scalar_max(
                out=tout2[b][:, 0:ncols], in0=tscr2[:, 0:ncols], scalar1=-1.0), "V")
            r.wait("gpsimd", tern_vtoks[i])
            tern_wtoks[i] = r.dma("gpsimd", lambda g, dst=dst, ncols=ncols, b=b: g.dma_start(
                out=dst, in_=tout2[b][:, 0:ncols]), "GD")
        tern_done = tern_wtoks[39]
        if "w1q" in taps:
            r.wait("sync", tern_done)
            tw = r.dma("sync", lambda s: s.dma_start(
                out=bvv(12288, 4096),
                in_=scr_W1q.rearrange("h p (dcc f) -> h p dcc f", f=128)[:, :, 0, :].transpose([1, 0, 2])))
            r.wait("vector", tw)
            tv = r.op("vector", lambda v: v.tensor_copy(out=tscr, in_=bvv(12288, 4096)), "V")
            r.wait("gpsimd", tv)
            tb = r.dma("gpsimd", lambda g: g.dma_start(out=tap_d["w1q"], in_=tscr), "GD")
            tap_barrier(tb)

        xchB = [fv(16384, 4096), fv(20480, 4096)]
        xnsc = fv(24576, 4096)
        xnown = fv(28672, 2048)
        Mx = fv(30720, 2048)
        XN2 = [bvv(dc * 2048, 4096) for dc in range(8)]

        r.wait("vector", bc_tok)
        r.wait("vector", tern_vtoks[39])
        r.wait("vector", tern_done)
        r.op("vector", lambda v: v.memset(Mx, 0.0))
        act_tok = None
        mx_toks = [None] * 8
        for dc in range(8):
            b = dc % 2
            if dc >= 2:
                r.wait("sync", mx_toks[dc - 2])
            if dc == 0:
                r.wait("sync", tern_vtoks[39])
            dtk = r.dma("sync", lambda s, dc=dc, b=b: s.dma_start(
                out=xchB[b], in_=x_t[dc * 128:(dc + 1) * 128, :]))
            r.wait("vector", dtk)
            if act_tok is not None:
                r.wait("vector", act_tok)
            if dc == 0:
                r.wait("vector", tern_vtoks[39])
                r.wait("vector", tern_done)
            r.op("vector", lambda v, b=b: v.tensor_mul(out=xnsc, in0=xchB[b], in1=A2bc))
            xn_tok = r.op("vector", lambda v: v.tensor_add(out=xnsc, in0=xnsc, in1=B2bc), "V")
            r.op("vector", lambda v, b=b: v.tensor_mul(out=xnown, in0=xchB[b][:, 0:NQ], in1=A1bc))
            r.op("vector", lambda v: v.tensor_add(out=xnown, in0=xnown, in1=B1bc))
            r.op("vector", lambda v: v.scalar_tensor_tensor(
                out=Mx, in0=xnown, scalar=-1.0, in1=Mx, op0=ALU.mult, op1=ALU.max))
            mx_toks[dc] = r.op("vector", lambda v: v.tensor_max(out=Mx, in0=Mx, in1=xnown), "V")
            r.wait("scalar", xn_tok)
            act_tok = r.op("scalar", lambda s, dc=dc: s.activation(
                out=XN2[dc], in_=xnsc, func=AF.Identity,
                bias=LNB[:, dc:dc + 1], scale=LNW[:, dc:dc + 1]), "S")
        xn2_done = act_tok
        if "xn2" in taps:
            r.wait("vector", xn2_done)
            tv = r.op("vector", lambda v: v.tensor_copy(out=xnsc, in_=XN2[0]), "V")
            r.wait("gpsimd", tv)
            tgx = r.dma("gpsimd", lambda g: g.dma_start(out=tap_d["xn2"], in_=xnsc), "GD")
            tap_barrier(tgx)

        t64 = fv(24576, 2048, 0, 64)
        r.wait("vector", xn2_done)
        r.op("vector", lambda v: v.tensor_copy(out=t64[0:64, :], in_=Mx[64:128, :]))
        r.op("vector", lambda v: v.tensor_tensor(out=Mx[0:64, :], in0=Mx[0:64, :], in1=t64[0:64, :], op=ALU.max))
        r.op("vector", lambda v: v.tensor_copy(out=t64[0:32, :], in_=Mx[32:64, :]))
        r.op("vector", lambda v: v.tensor_tensor(out=Mx[0:32, :], in0=Mx[0:32, :], in1=t64[0:32, :], op=ALU.max))
        r.op("vector", lambda v: v.transpose(out=t64[0:32, :], in_=Mx[0:32, :]))
        xs_col = sv(SCOL, 64)
        xqf_col = sv(SCOL + 64, 64)
        xf1_col = sv(SCOL + 128, 64)
        vop(lambda v: v.tensor_reduce(
            out=XS2[0:32, :], in_=t64[0:32, :].rearrange("p (b f) -> p b f", f=32),
            axis=AX.X, op=ALU.max))
        vop(lambda v: v.tensor_scalar_max(out=xs_col[0:32, :], in0=XS2[0:32, :], scalar1=EPS))
        vop(lambda v: v.reciprocal(out=XS3[0:32, :], in_=xs_col[0:32, :]))
        vop(lambda v: v.tensor_scalar_mul(out=xqf_col[0:32, :], in0=XS3[0:32, :], scalar1=127.0))
        xf_tok = vop(lambda v: v.tensor_scalar(
            out=xf1_col[0:32, :], in0=xs_col[0:32, :],
            scalar1=WSC[0:32, 0:1], scalar2=1.0 / 127.0, op0=ALU.mult, op1=ALU.mult))
        for slot, col in ((4, xqf_col), (5, xf1_col)):
            vt_tok = None
            for bj in range(2):
                vt_tok = vop(lambda v, col=col, bj=bj: v.transpose(
                    out=TCOL[0:32, bj * 32:(bj + 1) * 32],
                    in_=col[0:32, bj * 32:(bj + 1) * 32]))
            r.wait("gpsimd", vt_tok)
            def mk2(slot=slot):
                def f(g):
                    dst = scr_rows[slot:slot + 1, 0:NQ].rearrange(
                        "o (j a b) -> o j a b", j=2, a=32, b=32).transpose(
                        [0, 2, 1, 3]).squeeze(0)
                    return g.dma_start(out=dst, in_=TCOL[0:32, 0:64].rearrange(
                        "a (j b) -> a j b", b=32))
                return f
            stk = r.dma("gpsimd", mk2(), "GD")
            r.wait("vector", stk)
        xq_rows_tok = r.now("GD")
        XQFbc = fv(30720, 2048)
        r.wait("gpsimd", xq_rows_tok)
        xqbc_tok = r.dma("gpsimd", lambda g: g.dma_start(
            out=XQFbc, in_=scr_rows[4:5, 0:NQ].partition_broadcast(128)), "GD")

        XQB = [bvv(28672, 2048), bvv(29696, 2048)]
        xb2 = fv(24576, 2048)
        xn_t = fv(26624, 2048)
        xq_free_toks = [None] * 8
        xqw_toks = [None] * 8
        r.wait("sync", xf_tok)
        for dc in range(8):
            b = dc % 2
            if dc >= 1:
                r.wait("sync", xq_free_toks[dc - 1])
            dtk = r.dma("sync", lambda s, dc=dc: s.dma_start(
                out=xb2, in_=x_t[dc * 128:(dc + 1) * 128, 0:NQ]))
            r.wait("vector", dtk)
            if dc == 0:
                r.wait("vector", xqbc_tok)
            if dc >= 2:
                r.wait("vector", xqw_toks[dc - 2])
            r.op("vector", lambda v: v.tensor_mul(out=xn_t, in0=xb2, in1=A1bc))
            xq_free_toks[dc] = r.op("vector", lambda v: v.tensor_add(out=xn_t, in0=xn_t, in1=B1bc), "V")
            r.op("vector", lambda v: v.tensor_mul(out=xn_t, in0=xn_t, in1=XQFbc))
            xq_tok = r.op("vector", lambda v, b=b: v.tensor_scalar(
                out=XQB[b], in0=xn_t, scalar1=MAGIC, scalar2=MAGIC,
                op0=ALU.add, op1=ALU.subtract), "V")
            r.wait("gpsimd", xq_tok)
            xqw_toks[dc] = r.dma("gpsimd", lambda g, dc=dc, b=b: g.dma_start(
                out=scr_XQ[dc], in_=XQB[b]), "GD")
        xq_done = xqw_toks[7]
        if "xq" in taps:
            r.wait("sync", xq_done)
            tq1 = r.dma("sync", lambda s: s.dma_start(out=XQB[0], in_=scr_XQ[0]))
            r.wait("vector", tq1)
            tv = r.op("vector", lambda v: v.tensor_copy(out=xn_t, in_=XQB[0]), "V")
            r.wait("gpsimd", tv)
            tgq = r.dma("gpsimd", lambda g: g.dma_start(out=tap_d["xq"], in_=xn_t), "GD")
            tap_barrier(tgq)

        CK = bvv(24576, 4096); SK = bvv(26624, 4096)
        CQ = bvv(28672, 2048); SQ = bvv(29696, 2048)
        trsc = fv(30720, 2048, 0, 32)
        KpreP = [bvv(32768, 4096), bvv(16384, 4096)]
        QpreP = [bvv(38912, 2048), bvv(18432, 2048)]
        swap = bvv(34816, 4096)
        m1 = bvv(36864, 2048); m2 = bvv(37888, 2048)
        VTMa = bvv(34816, 2080)
        VTMb = bvv(36864, 2080)
        TSTG = [bvv(39936, 64), bvv(39968, 64)]
        WSTGP = [[fv(40960, 1024), fv(41984, 1024), fv(43008, 1024)],
                 [fv(19456, 1024), fv(20480, 1024), fv(21504, 1024)]]
        WBP = [[bvv(44032, 1024), bvv(44544, 1024), bvv(45056, 1024)],
               [bvv(22528, 1024), bvv(23040, 1024), bvv(23552, 1024)]]
        trsc2 = fv(45568, 2048, 0, 32)

        r.wait("sync", xq_done)
        ld_ck = r.dma("sync", lambda s: s.dma_start(out=CK, in_=ropeCK))
        ld_sk = r.dma("sync", lambda s: s.dma_start(out=SK, in_=ropeSK))
        ld_cq = r.dma("sync", lambda s: s.dma_start(out=CQ, in_=ropeCQ))
        ld_sq = r.dma("sync", lambda s: s.dma_start(out=SQ, in_=ropeSQ))
        for tok in (ld_ck, ld_sk, ld_cq, ld_sq):
            r.wait("vector", tok)
        r.wait("vector", xqbc_tok)

        kq_scr_tok = {}
        v_scr_tok = {}
        c_dma_tok = None
        wcast_by_par = [None, None]
        vtrans_by_par = [None, None]
        qfree_by_par = [None, None]
        prev_pair_scr = None

        def gen_fm(wb, bias_col, ntok, outbuf, first_mm_wait, evac_wait):
            evacs = []
            for ntc in range(ntok // 512):
                ps = PSX if ntc % 2 == 0 else PSY
                if ntc == 0 and first_mm_wait is not None:
                    r.wait("tensor", first_mm_wait)
                if len(evacs) >= 2:
                    r.wait("tensor", evacs[-2])
                tk = None
                for dc in range(8):
                    tk = r.op("tensor", lambda t, dc=dc, ntc=ntc, ps=ps, wb=wb: t.matmul(
                        out=ps[:, 0:512], lhsT=wb[:, dc * 128:(dc + 1) * 128],
                        rhs=XN2[dc][:, ntc * 512:(ntc + 1) * 512],
                        start=(dc == 0), stop=(dc == 7)), "T" if dc == 7 else None)
                r.wait("scalar", tk)
                if ntc == 0 and evac_wait is not None:
                    r.wait("scalar", evac_wait)
                evacs.append(r.op("scalar", lambda s, ntc=ntc, ps=ps, bias_col=bias_col, outbuf=outbuf: s.activation(
                    out=outbuf[:, ntc * 512:(ntc + 1) * 512], in_=ps[:, 0:512],
                    func=AF.Identity, bias=bias_col, scale=1.0), "S"))
            return evacs[-1]

        for p in range(NPAIR):
            par = p % 2
            Kpre = KpreP[par]
            Qpre = QpreP[par]
            V_fm = Kpre
            WSTG = WSTGP[par]
            WB = WBP[par]
            if wcast_by_par[par] is not None:
                r.wait("sync", wcast_by_par[par])
            wtoks = []
            for wi, wsrc in enumerate((wqT, wkT, wvT)):
                def mkw(wi=wi, wsrc=wsrc, p=p, WSTG=WSTG):
                    def f(s):
                        src = wsrc[:, p * 128:(p + 1) * 128].rearrange("(dc q) f -> q dc f", q=128)
                        return s.dma_start(out=WSTG[wi].rearrange("q (dc f) -> q dc f", f=128), in_=src)
                    return f
                wtoks.append(r.dma("sync", mkw()))
            casts = []
            for wi in range(3):
                r.wait("vector", wtoks[wi])
                casts.append(r.op("vector", lambda v, wi=wi, WB=WB, WSTG=WSTG: v.tensor_copy(out=WB[wi], in_=WSTG[wi]), "V"))
            wcast_by_par[par] = casts[-1]

            k_evac = gen_fm(WB[1], BKC[:, p:p + 1], SEQ, Kpre, casts[-1], vtrans_by_par[par])
            r.wait("vector", k_evac)
            if prev_pair_scr is not None:
                r.wait("vector", prev_pair_scr)
            for h4 in range(4):
                src = (32, 0, 96, 64)[h4]
                r.op("vector", lambda v, h4=h4, src=src, Kpre=Kpre: v.tensor_copy(
                    out=swap[h4 * 32:(h4 + 1) * 32, :], in_=Kpre[src:src + 32, :]))
            for hh in range(2):
                sl = slice(hh * 2048, (hh + 1) * 2048)
                r.op("vector", lambda v, sl=sl, Kpre=Kpre: v.tensor_tensor(out=m1, in0=Kpre[:, sl], in1=CK[:, sl], op=ALU.mult))
                r.op("vector", lambda v, sl=sl: v.tensor_tensor(out=m2, in0=swap[:, sl], in1=SK[:, sl], op=ALU.mult))
                r.op("vector", lambda v, sl=sl, Kpre=Kpre: v.tensor_add(out=Kpre[:, sl], in0=m1, in1=m2))
            krope_tok = r.op("vector", lambda v: v.engine_nop(), "V")

            q_evac = gen_fm(WB[0], BQC[:, p:p + 1], NQ, Qpre, krope_tok, qfree_by_par[par])
            r.wait("vector", q_evac)
            for h4 in range(4):
                src = (32, 0, 96, 64)[h4]
                r.op("vector", lambda v, h4=h4, src=src, Qpre=Qpre: v.tensor_copy(
                    out=swap[h4 * 32:(h4 + 1) * 32, 0:2048], in_=Qpre[src:src + 32, :]))
            r.op("vector", lambda v, Qpre=Qpre: v.tensor_tensor(out=m1, in0=Qpre, in1=CQ, op=ALU.mult))
            r.op("vector", lambda v: v.tensor_tensor(out=m2, in0=swap[:, 0:2048], in1=SQ, op=ALU.mult))
            qrope_tok = r.op("vector", lambda v, Qpre=Qpre: v.tensor_add(out=Qpre, in0=m1, in1=m2), "V")

            ctok = None
            for h01 in range(2):
                r.wait("tensor", qrope_tok)
                if ctok is not None:
                    r.wait("tensor", ctok)
                stok = None
                for qq in range(4):
                    stok = r.op("tensor", lambda t, h01=h01, qq=qq, Kpre=Kpre, Qpre=Qpre: t.matmul(
                        out=PS_BIG[0:64, qq * 512:(qq + 1) * 512],
                        lhsT=Kpre[h01 * 64:h01 * 64 + 64, 32:4096:64],
                        rhs=Qpre[h01 * 64:h01 * 64 + 64, qq * 512:(qq + 1) * 512],
                        start=True, stop=True, tile_position=(h01 * 64, 0)),
                        "T" if qq == 3 else None)
                r.wait("vector", stok)
                vop(lambda v: v.tensor_copy(out=trsc2[0:32, :], in_=PS_BIG[32:64, :]))
                vop(lambda v: v.tensor_tensor(
                    out=trsc[0:32, :], in0=PS_BIG[0:32, :], in1=trsc2[0:32, :], op=ALU.max))
                vop(lambda v: v.transpose(out=trsc2[0:32, :], in_=trsc[0:32, :]))
                vop(lambda v: v.memset(NEGC, -1e30))
                vop(lambda v: v.tensor_reduce(
                    out=NEGC[0:32, 0:4],
                    in_=trsc2[0:32, :].rearrange("p (qq bf) -> p qq bf", qq=4),
                    axis=AX.X, op=ALU.max))
                vop(lambda v: v.transpose(out=NEGCT, in_=NEGC))
                vop(lambda v: v.tensor_reduce(out=CF2, in_=NEGCT, axis=AX.X, op=ALU.max))
                ctok = vop(lambda v: v.tensor_scalar_mul(out=CFIN, in0=CF2, scalar1=-1.0))
                r.wait("gpsimd", ctok)
                head = 2 * p + h01
                c_dma_tok = r.dma("gpsimd", lambda g, head=head: g.dma_start(
                    out=scr_c[0:1, head * 4:head * 4 + 4], in_=CFIN[0:4, 0:1]), "GD")

            r.wait("gpsimd", qrope_tok)
            kq_scr_tok[p] = [
                r.dma("gpsimd", lambda g, p=p, Kpre=Kpre: g.dma_start(out=scr_K[p], in_=Kpre), "GD"),
                r.dma("gpsimd", lambda g, p=p, Qpre=Qpre: g.dma_start(out=scr_Q[p], in_=Qpre), "GD"),
            ]

            v_evac = gen_fm(WB[2], BVC[:, p:p + 1], SEQ, V_fm, None, kq_scr_tok[p][1])
            r.wait("vector", v_evac)
            o65a = r.op("vector", lambda v: v.memset(
                VTMa.rearrange("q (tt c) -> q tt c", c=65)[:, :, 64:65], 1.0))
            o65b = r.op("vector", lambda v: v.memset(
                VTMb.rearrange("q (tt c) -> q tt c", c=65)[:, :, 64:65], 1.0), "V")
            r.wait("scalar", v_evac)
            if prev_pair_scr is not None:
                r.wait("scalar", prev_pair_scr)
            tcopies = []
            for h01 in range(2):
                vtm = VTMa if h01 == 0 else VTMb
                for tt in range(32):
                    sl = TSTG[tt % 2]
                    idx = h01 * 32 + tt
                    q = "scalar"
                    if idx >= 2:
                        r.wait(q, tcopies[idx - 2])
                    ttok = r.dma(q, lambda s, h01=h01, tt=tt, sl=sl, V_fm=V_fm: s.dma_start_transpose(
                        out=sl, in_=V_fm[h01 * 64:(h01 + 1) * 64, tt * 128:(tt + 1) * 128]),
                        "SD" if q == "scalar" else "LD")
                    r.wait("vector", ttok)
                    if idx == 0:
                        r.wait("vector", o65b)
                    tcopies.append(r.op("vector", lambda v, vtm=vtm, tt=tt, sl=sl: v.tensor_copy(
                        out=vtm[:, tt * 65:tt * 65 + 64], in_=sl), "V"))
                head = 2 * p + h01
                r.wait("gpsimd", tcopies[-1])
                def mkv(head=head, vtm=vtm):
                    def f(g):
                        return g.dma_start(out=scr_V[head].transpose([1, 0, 2]),
                                           in_=vtm.rearrange("q (tt c) -> q tt c", c=65))
                    return f
                v_scr_tok[head] = r.dma("gpsimd", mkv(), "GD")
            prev_pair_scr = v_scr_tok[2 * p + 1]
            vtrans_by_par[par] = v_scr_tok[2 * p + 1]
            qfree_by_par[par] = kq_scr_tok[p][1]

        c_all_done = prev_pair_scr
        r.wait("gpsimd", c_dma_tok)
        cbias_tok = r.dma("gpsimd", lambda g: g.dma_start(
            out=CBIAS, in_=scr_c[0:1, 0:64].partition_broadcast(128)), "GD")
        if "cneg" in taps:
            r.wait("gpsimd", cbias_tok)
            tb = r.dma("gpsimd", lambda g: g.dma_start(out=tap_d["cneg"], in_=scr_c), "GD")
            tap_barrier(tb)

        KB = [bvv(0, 4096), bvv(2048, 4096)]
        QB = [bvv(4096, 2048), bvv(5120, 2048)]
        VA = [bvv(6144, 2080), bvv(8224, 2080)]
        VB = [bvv(7184, 2080), bvv(9264, 2080)]
        Pb = [bvv(10304, 2048), bvv(11328, 2048)]
        PVS_A = fv(12352, 2048, 0, 64)
        PVS_B = fv(32768, 2048, 0, 64)
        RLBC_A = fv(34816, 2048, 0, 64)
        RLBC_B = fv(36864, 2048, 0, 64)
        LR_A = fv(38912, 2048, 0, 1)
        LRr_A = fv(38912, 2048, 32, 33)
        LR_B = fv(38912, 2048, 64, 65)
        LRr_B = fv(38912, 2048, 96, 97)
        BST = bvv(40960, 2048)
        ATT = [bvv(24576 + pp * 1024, 2048) for pp in range(NPAIR)]

        def load_pair(p, parity, buf_free_tok):
            r.wait("sync", v_scr_tok[2 * p + 1])
            r.wait("sync", c_all_done)
            if buf_free_tok is not None:
                r.wait("sync", buf_free_tok)
            return [
                r.dma("sync", lambda s, p=p, parity=parity: s.dma_start(out=KB[parity], in_=scr_K[p])),
                r.dma("sync", lambda s, p=p, parity=parity: s.dma_start(out=QB[parity], in_=scr_Q[p])),
                r.dma("sync", lambda s, p=p, parity=parity: s.dma_start(
                    out=VA[parity].rearrange("q (tt c) -> q tt c", c=65),
                    in_=scr_V[2 * p].transpose([1, 0, 2]))),
                r.dma("sync", lambda s, p=p, parity=parity: s.dma_start(
                    out=VB[parity].rearrange("q (tt c) -> q tt c", c=65),
                    in_=scr_V[2 * p + 1].transpose([1, 0, 2]))),
            ]

        r.wait("tensor", cbias_tok)
        pair_load_toks = load_pair(0, 0, None)
        pair_last_T = {}
        pb_free = [None, None]
        sc_free = [None, None]
        last_exp = None
        bank_free = {0: None, 1: None}
        prev_att_mults = None
        att_done_tok = None

        for p in range(NPAIR):
            parity = p % 2
            if p + 1 < NPAIR:
                next_load = load_pair(p + 1, 1 - parity, pair_last_T.get(p - 1))
            for tok in pair_load_toks:
                r.wait("tensor", tok)
            evac_v_tok = None
            for qc in range(4):
                PVq = [PV0, PV1] if qc % 2 == 0 else [PSX, PSY]
                if bank_free[qc % 2] is not None:
                    r.wait("tensor", bank_free[qc % 2])
                exp_toks = {}
                pv_tok2 = [None, None]
                for u in range(17):
                    for h01 in range(2):
                        em = 2 * u + h01
                        slot = em % 2
                        if u < 16:
                            r.wait("tensor", sc_free[slot])
                            sc_tok = None
                            for kt in range(2):
                                ki = u * 2 + kt
                                sc_tok = r.op("tensor", lambda t, h01=h01, kt=kt, ki=ki, qc=qc, parity=parity, hbase=slot * 1024: t.matmul(
                                    out=PS_BIG[:, hbase + kt * 512:hbase + (kt + 1) * 512],
                                    lhsT=KB[parity][h01 * 64:(h01 + 1) * 64, ki * 128:(ki + 1) * 128],
                                    rhs=QB[parity][h01 * 64:(h01 + 1) * 64, qc * 512:(qc + 1) * 512],
                                    start=True, stop=True, tile_position=(h01 * 64, 0)),
                                    "T" if kt == 1 else None)
                            r.wait("scalar", sc_tok)
                            if pb_free[slot] is not None:
                                r.wait("scalar", pb_free[slot])
                            ci = (2 * p + h01) * 4 + qc
                            exp_toks[em] = r.op("scalar", lambda s, slot=slot, ci=ci, hbase=slot * 1024: s.activation(
                                out=Pb[slot][:, 0:1024], in_=PS_BIG[:, hbase:hbase + 1024], func=AF.Exp,
                                bias=CBIAS[:, ci:ci + 1], scale=1.0), "S")
                            sc_free[slot] = exp_toks[em]
                        if u >= 1:
                            up = u - 1
                            em_p = 2 * up + h01
                            pslot = em_p % 2
                            r.wait("tensor", exp_toks[em_p])
                            for kt in range(2):
                                ki = up * 2 + kt
                                vbuf = VA[parity] if h01 == 0 else VB[parity]
                                pv_tok2[h01] = r.op("tensor", lambda t, PVps=PVq[h01], vbuf=vbuf, ki=ki, kt=kt, pslot=pslot: t.matmul(
                                    out=PVps[0:65, :],
                                    lhsT=vbuf[:, ki * 65:ki * 65 + 65],
                                    rhs=Pb[pslot][:, kt * 512:(kt + 1) * 512],
                                    start=(ki == 0), stop=(ki == 31)),
                                    "T" if kt == 1 else None)
                            pb_free[pslot] = pv_tok2[h01]
                for h01 in range(2):
                    PVps = PVq[h01]
                    r.wait("vector", pv_tok2[h01])
                    PVS = PVS_A if h01 == 0 else PVS_B
                    LR = LR_A if h01 == 0 else LR_B
                    if prev_att_mults is not None and qc == 0 and h01 == 0:
                        r.wait("vector", prev_att_mults)
                    r.op("vector", lambda v, PVS=PVS, qc=qc, PVps=PVps: v.tensor_copy(
                        out=PVS[0:64, qc * 512:(qc + 1) * 512], in_=PVps[0:64, :]))
                    evac_v_tok = r.op("vector", lambda v, LR=LR, qc=qc, PVps=PVps: v.tensor_copy(
                        out=LR[:, qc * 512:(qc + 1) * 512], in_=PVps[64:65, :]), "V")
                bank_free[qc % 2] = evac_v_tok
            r.op("vector", lambda v: v.reciprocal(out=LRr_A, in_=LR_A))
            rl_tok = r.op("vector", lambda v: v.reciprocal(out=LRr_B, in_=LR_B), "V")
            r.wait("gpsimd", rl_tok)
            r.dma("gpsimd", lambda g: g.dma_start(out=scr_rows[7:8, 0:NQ], in_=LRr_A), "GD")
            w2tk = r.dma("gpsimd", lambda g: g.dma_start(out=scr_rows[8:9, 0:NQ], in_=LRr_B), "GD")
            r.wait("gpsimd", w2tk)
            r.dma("gpsimd", lambda g: g.dma_start(out=RLBC_A, in_=scr_rows[7:8, 0:NQ].partition_broadcast(64)), "GD")
            rb_tok = r.dma("gpsimd", lambda g: g.dma_start(out=RLBC_B, in_=scr_rows[8:9, 0:NQ].partition_broadcast(64)), "GD")
            r.wait("vector", rb_tok)
            r.op("vector", lambda v, p=p: v.tensor_tensor(
                out=ATT[p][0:64, :], in0=PVS_A[0:64, :], in1=RLBC_A[0:64, :], op=ALU.mult))
            r.op("vector", lambda v: v.tensor_tensor(
                out=BST[0:64, :], in0=PVS_B[0:64, :], in1=RLBC_B[0:64, :], op=ALU.mult))
            prev_att_mults = r.op("vector", lambda v, p=p: v.tensor_copy(
                out=ATT[p][64:128, :], in_=BST[0:64, :]), "V")
            pair_last_T[p] = r.now("T")
            if p + 1 < NPAIR:
                pair_load_toks = next_load
        att_done_tok = prev_att_mults
        if "att" in taps:
            r.wait("vector", att_done_tok)
            tvv = r.op("vector", lambda v: v.tensor_copy(out=PVS_A[0:64, :], in_=ATT[0][0:64, :]))
            tvv = r.op("vector", lambda v: v.tensor_copy(out=fv(12352, 2048, 64, 128), in_=ATT[0][64:128, :]), "V")
            r.wait("gpsimd", tvv)
            tg = r.dma("gpsimd", lambda g: g.dma_start(out=tap_d["att"], in_=fv(12352, 2048)), "GD")
            tap_barrier(tg)
        if "pvs" in taps:
            r.wait("gpsimd", att_done_tok)
            tb = r.dma("gpsimd", lambda g: g.dma_start(out=tap_d["pvs"], in_=PVS_B[0:64, :]), "GD")
            tap_barrier(tb)
        if "lr" in taps:
            r.wait("gpsimd", att_done_tok)
            tb = r.dma("gpsimd", lambda g: g.dma_start(out=tap_d["lr"], in_=LR_B), "GD")
            tap_barrier(tb)

        H = [fv(ht * 512, 512) for ht in range(32)]
        XQS = [[bvv(16384 + par * 2048 + dc * 256, 512) for dc in range(8)] for par in range(2)]
        HQ = [bvv(ht * 512, 512) for ht in range(32)]
        XF1bc = fv(44032, 2048)
        QFbc = fv(46080, 512)
        HF2bc = fv(46592, 512)
        W1t = [bvv(16384 - 0, 0), bvv(0, 0)]
        hs_col = sv(SCOL, 16)
        qf_col = sv(SCOL + 64, 16)
        hf2_col = sv(SCOL + 128, 16)

        r.wait("gpsimd", att_done_tok)
        xf1bc_tok = r.dma("gpsimd", lambda g: g.dma_start(
            out=XF1bc, in_=scr_rows[5:6, 0:NQ].partition_broadcast(128)), "GD")
        r.wait("sync", att_done_tok)
        r.wait("sync", tern_done)

        W1t = [SM_BIG_B[:, 0:1024], SM_BIG_B[:, 1024:2048]]
        W2t = [SM_BIG_B[:, 2048:3072], SM_BIG_B[:, 3072:4096]]
        DQ = [SM_BIG[:, 2048:2560], SM_BIG[:, 2560:3072]]
        Mh = SM_BIG[:, 3072:3584]
        TR1 = SM_BIG[0:64, 3584:4096]
        DQ2 = [SM_BIG[:, 3072:3584], SM_BIG[:, 3584:4096]]
        HQS = SM_BIG_B[:, 4096:4608]
        TR2 = SM_BIG[0:32, 2048:2560]

        gelu_slot = [None, None]
        deq_slot = [None, None]
        mm1_slot = [None, None]
        hq_prev_tc = None
        mm2_last_tc = None
        g_done = None
        xqs_toks = [None] * 4
        mm1_done_tc = [None] * 4
        r.wait("sync", xq_done)
        for dc in range(8):
            xqs_toks[0] = r.dma("sync", lambda s, dc=dc: s.dma_start(
                out=XQS[0][dc], in_=scr_XQ[dc][:, 0:512]))
        for tc in range(4):
            par = tc % 2
            csl = slice(tc * 512, (tc + 1) * 512)
            if tc + 1 < 4:
                if tc >= 1:
                    r.wait("sync", mm1_done_tc[tc - 1])
                for dc in range(8):
                    xqs_toks[tc + 1] = r.dma("sync", lambda s, dc=dc, tc=tc, par=par: s.dma_start(
                        out=XQS[1 - par][dc], in_=scr_XQ[dc][:, (tc + 1) * 512:(tc + 2) * 512]))
            r.wait("vector", mm2_last_tc)
            r.wait("vector", g_done)
            mh_tok = r.op("vector", lambda v: v.memset(Mh, 0.0), "V")
            for ht in range(32):
                wslot = ht % 2
                wb = W1t[wslot]
                if mm1_slot[wslot] is not None:
                    r.wait("sync", mm1_slot[wslot])
                dtk = r.dma("sync", lambda s, ht=ht, wb=wb: s.dma_start(out=wb, in_=scr_W1q[ht]))
                r.wait("tensor", dtk)
                if ht == 0:
                    r.wait("tensor", xqs_toks[tc])
                ps = PS_BIG[:, 0:512] if ht % 2 == 0 else PS_BIG[:, 1024:1536]
                if deq_slot[wslot] is not None:
                    r.wait("tensor", deq_slot[wslot])
                mmtok = None
                for dc in range(8):
                    mmtok = r.op("tensor", lambda t, dc=dc, wb=wb, ps=ps, par=par: t.matmul(
                        out=ps, lhsT=wb[:, dc * 128:(dc + 1) * 128],
                        rhs=XQS[par][dc], start=(dc == 0), stop=(dc == 7)),
                        "T" if dc == 7 else None)
                mm1_slot[wslot] = mmtok
                if ht == 31:
                    mm1_done_tc[tc] = mmtok
                r.wait("vector", mmtok)
                if tc == 0 and ht == 0:
                    r.wait("vector", xf1bc_tok)
                dq = DQ[wslot]
                if gelu_slot[wslot] is not None:
                    r.wait("vector", gelu_slot[wslot])
                if hq_prev_tc is not None and ht < 2:
                    r.wait("scalar", hq_prev_tc)
                if mm2_last_tc is not None and ht < 2:
                    r.wait("scalar", mm2_last_tc)
                deq_slot[wslot] = r.op("vector", lambda v, dq=dq, ps=ps, csl=csl: v.tensor_tensor(
                    out=dq, in0=ps, in1=XF1bc[:, csl], op=ALU.mult), "V")
                r.wait("scalar", deq_slot[wslot])
                gelu_slot[wslot] = r.op("scalar", lambda s, ht=ht, dq=dq: s.activation(
                    out=H[ht], in_=dq, func=AF.Gelu, bias=B1C[:, ht:ht + 1], scale=1.0), "S")
                r.wait("vector", gelu_slot[wslot])
                r.op("vector", lambda v, ht=ht: v.scalar_tensor_tensor(
                    out=Mh, in0=H[ht], scalar=-1.0, in1=Mh, op0=ALU.mult, op1=ALU.max))
                r.op("vector", lambda v, ht=ht: v.tensor_max(out=Mh, in0=Mh, in1=H[ht]))
            r.op("vector", lambda v: v.tensor_copy(out=TR1[0:64, :], in_=Mh[64:128, :]))
            r.op("vector", lambda v: v.tensor_tensor(out=Mh[0:64, :], in0=Mh[0:64, :], in1=TR1[0:64, :], op=ALU.max))
            r.op("vector", lambda v: v.tensor_copy(out=TR2[0:32, :], in_=Mh[32:64, :]))
            r.op("vector", lambda v: v.tensor_tensor(out=Mh[0:32, :], in0=Mh[0:32, :], in1=TR2[0:32, :], op=ALU.max))
            r.op("vector", lambda v: v.transpose(out=TR2[0:32, :], in_=Mh[0:32, :]))
            vop(lambda v: v.tensor_reduce(
                out=XS2[0:32, 0:16], in_=TR2[0:32, :].rearrange("p (b f) -> p b f", f=32),
                axis=AX.X, op=ALU.max))
            vop(lambda v: v.tensor_scalar_max(out=hs_col[0:32, :], in0=XS2[0:32, 0:16], scalar1=EPS))
            vop(lambda v: v.reciprocal(out=XS3[0:32, 0:16], in_=hs_col[0:32, :]))
            vop(lambda v: v.tensor_scalar_mul(out=qf_col[0:32, :], in0=XS3[0:32, 0:16], scalar1=127.0))
            hf_tok = vop(lambda v: v.tensor_scalar(
                out=hf2_col[0:32, :], in0=hs_col[0:32, :],
                scalar1=WSC[0:32, 2:3], scalar2=1.0 / 127.0, op0=ALU.mult, op1=ALU.mult))
            r.wait("gpsimd", hf_tok)
            for slot, col in ((9, qf_col), (10, hf2_col)):
                colw = sv(SCOL + 64, 32) if col is qf_col else sv(SCOL + 128, 32)
                vt_tok = vop(lambda v, colw=colw: v.transpose(
                    out=TCOL[0:32, 96:128], in_=colw[0:32, :]))
                r.wait("gpsimd", vt_tok)
                def mk3(slot=slot, tc=tc):
                    def f(g):
                        dst = scr_rows[slot:slot + 1, tc * 512:(tc + 1) * 512].rearrange(
                            "o (b p) -> o b p", p=32).squeeze(0)
                        return g.dma_start(out=dst, in_=TCOL[0:16, 96:128])
                    return f
                stk = r.dma("gpsimd", mk3(), "GD")
                r.wait("vector", stk)
            hrow_tok = r.now("GD")
            r.wait("gpsimd", hrow_tok)
            r.dma("gpsimd", lambda g, tc=tc: g.dma_start(
                out=QFbc, in_=scr_rows[9:10, tc * 512:(tc + 1) * 512].partition_broadcast(128)), "GD")
            hbc_tok = r.dma("gpsimd", lambda g, tc=tc: g.dma_start(
                out=HF2bc, in_=scr_rows[10:11, tc * 512:(tc + 1) * 512].partition_broadcast(128)), "GD")
            r.wait("vector", hbc_tok)
            hq_last = None
            for ht in range(32):
                r.op("vector", lambda v, ht=ht: v.tensor_mul(out=H[ht], in0=H[ht], in1=QFbc))
                r.op("vector", lambda v, ht=ht: v.tensor_scalar(
                    out=HQS, in0=H[ht], scalar1=MAGIC, scalar2=MAGIC,
                    op0=ALU.add, op1=ALU.subtract))
                hq_last = r.op("vector", lambda v, ht=ht: v.tensor_copy(
                    out=HQ[ht], in_=HQS), "V" if ht == 31 else None)
            hq_prev_tc = hq_last
            if "h0" in taps and tc == 0:
                r.wait("gpsimd", hq_last)
                tb = r.dma("gpsimd", lambda g: g.dma_start(out=tap_d["h0"], in_=H[0]), "GD")
                tap_barrier(tb)
            mm2_tok = None
            sil_slot = [None, None]
            deq2_slot = [None, None]
            w2_dma_slot = [None, None]
            for dt in range(8):
                psl = dt % 2
                ps = PSX[:, 0:512] if psl == 0 else PSY[:, 0:512]
                if deq2_slot[psl] is not None:
                    r.wait("tensor", deq2_slot[psl])
                for grp in range(4):
                    gslot = grp % 2
                    wb = W2t[gslot]
                    if w2_dma_slot[gslot] is not None:
                        r.wait("sync", w2_dma_slot[gslot])
                    d2 = r.dma("sync", lambda s, dt=dt, grp=grp, wb=wb: s.dma_start(
                        out=wb, in_=scr_W2q[dt, grp]))
                    r.wait("tensor", d2)
                    if dt == 0 and grp == 0:
                        r.wait("tensor", hq_last)
                    for j in range(8):
                        hc = grp * 8 + j
                        mm2_tok = r.op("tensor", lambda t, j=j, hc=hc, wb=wb, ps=ps: t.matmul(
                            out=ps, lhsT=wb[:, j * 128:(j + 1) * 128],
                            rhs=HQ[hc], start=(hc == 0), stop=(hc == 31)),
                            "T" if j == 7 else None)
                    w2_dma_slot[gslot] = mm2_tok
                r.wait("vector", mm2_tok)
                dq = DQ2[psl]
                if sil_slot[psl] is not None:
                    r.wait("vector", sil_slot[psl])
                deq2_slot[psl] = r.op("vector", lambda v, dq=dq, ps=ps: v.tensor_tensor(
                    out=dq, in0=ps, in1=HF2bc, op=ALU.mult), "V")
                r.wait("scalar", deq2_slot[psl])
                sil_slot[psl] = r.op("scalar", lambda s, dt=dt, dq=dq, csl=csl: s.activation(
                    out=bvv(32768 + dt * 1024, 2048)[:, csl], in_=dq,
                    func=AF.Silu, bias=B2C[:, dt:dt + 1], scale=1.0), "S")
            mm2_last_tc = r.now("T")
            g_done = sil_slot[(8 - 1) % 2]
        G = [bvv(32768 + dt * 1024, 2048) for dt in range(8)]

        s_chunk = fv(0, 2048)
        xn_chunk = fv(2048, 2048)
        o_chunk = fv(4096, 2048)
        xF = fv(6144, 2048)
        wo_stage = fv(8192, 1024)
        WOB = [bvv(9216 + ac * 512, 1024) for ac in range(8)]
        A1b2 = fv(13312, 2048)
        B1b2 = fv(15360, 2048)

        r.wait("gpsimd", g_done)
        r.dma("gpsimd", lambda g: g.dma_start(out=A1b2, in_=scr_rows[2:3, 0:NQ].partition_broadcast(128)), "GD")
        fbc_tok = r.dma("gpsimd", lambda g: g.dma_start(out=B1b2, in_=scr_rows[3:4, 0:NQ].partition_broadcast(128)), "GD")

        r.wait("sync", g_done)
        wo_cast = None
        for ac in range(8):
            if wo_cast is not None:
                r.wait("sync", wo_cast)
            dtk = r.dma("sync", lambda s, ac=ac: s.dma_start(
                out=wo_stage, in_=woT[ac * 128:(ac + 1) * 128, :]))
            r.wait("vector", dtk)
            wo_cast = r.op("vector", lambda v, ac=ac: v.tensor_copy(out=WOB[ac], in_=wo_stage), "V")

        st_count = 0
        prev_out_v = None
        prev_evacs = []
        last_psum_user_v = r.now("V")
        for dt in range(8):
            if prev_out_v is not None:
                r.wait("sync", prev_out_v)
            xtk = r.dma("sync", lambda s, dt=dt: s.dma_start(
                out=xF, in_=x_t[dt * 128:(dt + 1) * 128, 0:NQ]))
            evac_toks = []
            for qc in range(4):
                ps = PSX if qc % 2 == 0 else PSY
                if dt == 0 and qc == 0:
                    r.wait("tensor", wo_cast)
                    r.wait("tensor", last_psum_user_v)
                idx = dt * 4 + qc
                allev = prev_evacs + evac_toks
                if idx >= 2:
                    r.wait("tensor", allev[idx - 2])
                wo_mm = None
                for ac in range(8):
                    wo_mm = r.op("tensor", lambda t, ac=ac, qc=qc, dt=dt, ps=ps: t.matmul(
                        out=ps[:, 0:512], lhsT=WOB[ac][:, dt * 128:(dt + 1) * 128],
                        rhs=ATT[ac][:, qc * 512:(qc + 1) * 512],
                        start=(ac == 0), stop=(ac == 7)), "T" if ac == 7 else None)
                r.wait("scalar", wo_mm)
                if dt > 0 and qc == 0:
                    r.wait("scalar", prev_out_v)
                ev = r.op("scalar", lambda s, qc=qc, dt=dt, ps=ps: s.activation(
                    out=s_chunk[:, qc * 512:(qc + 1) * 512], in_=ps[:, 0:512],
                    func=AF.Identity, bias=BOC[:, dt:dt + 1], scale=1.0), "S")
                evac_toks.append(ev)
            prev_evacs = prev_evacs + evac_toks
            r.wait("vector", evac_toks[-1])
            r.wait("vector", xtk)
            if dt == 0:
                r.wait("vector", fbc_tok)
            if prev_out_v is not None:
                r.wait("vector", ("ST", st_count))
            r.op("vector", lambda v: v.tensor_mul(out=xn_chunk, in0=xF, in1=A1b2))
            r.op("vector", lambda v: v.tensor_add(out=xn_chunk, in0=xn_chunk, in1=B1b2))
            r.op("vector", lambda v: v.tensor_add(out=s_chunk, in0=s_chunk, in1=xn_chunk))
            r.op("vector", lambda v, dt=dt: v.tensor_tensor(out=o_chunk, in0=s_chunk, in1=G[dt], op=ALU.mult))
            prev_out_v = r.op("vector", lambda v: v.tensor_add(out=o_chunk, in0=o_chunk, in1=xF), "V")
            r.wait("sync", prev_out_v)
            r.dma("sync", lambda s, dt=dt: s.dma_start(
                out=out_d[dt * 128:(dt + 1) * 128, :], in_=o_chunk), "ST")
            st_count += 16
        r.wait("sync", ("ST", st_count))

        for name in ("sync", "gpsimd", "tensor", "vector", "scalar"):
            items = r.q[name]
            getattr(block, name)(lambda eng, items=items: _replay(eng, items, sems))

    return nc


_NC_CACHE = {}


def _get_nc(taps=()):
    key = tuple(sorted(taps))
    if key not in _NC_CACHE:
        _NC_CACHE[key] = build(taps)
    return _NC_CACHE[key]


def _rope_tables():
    i = np.arange(0, 64, 2, dtype=np.float32) / np.float32(64.0)
    inv_freq = (np.float32(1.0) / (np.float32(10000.0) ** i)).astype(np.float32)
    t = np.arange(SEQ, dtype=np.float32)
    ang = np.outer(t, inv_freq).astype(np.float32)
    cos = np.cos(ang).astype(np.float32).T
    sin = np.sin(ang).astype(np.float32).T
    C = np.tile(cos, (4, 1))
    S = np.concatenate([-sin, sin, -sin, sin], axis=0)
    return C, S


def make_in_maps(x, ln_w, ln_b, Wq, bq, Wk, bk, Wv, bv, Wo, bo, W1, b1, W2, b2):
    perm = np.concatenate([h * 64 + np.concatenate([np.arange(0, 64, 2), np.arange(1, 64, 2)])
                           for h in range(NHEAD)]).astype(np.int64)
    asc = np.ascontiguousarray
    f32 = np.float32
    bf = ml_dtypes.bfloat16
    wqT_ = asc(np.asarray(Wq, f32).T[:, perm])
    wkT_ = asc(np.asarray(Wk, f32).T[:, perm])
    wvT_ = asc(np.asarray(Wv, f32).T)
    woT_ = asc(np.asarray(Wo, f32).T)
    col8 = lambda a: asc(np.asarray(a, f32).reshape(8, 128).T)
    col32 = lambda a: asc(np.asarray(a, f32).reshape(32, 128).T)
    shared = dict(
        wqT=wqT_, wkT=wkT_, wvT=wvT_, woT=woT_,
        bqc=col8(np.asarray(bq, f32)[perm]), bkc=col8(np.asarray(bk, f32)[perm]),
        bvc=col8(bv), boc=col8(bo), lnwc=col8(ln_w), lnbc=col8(ln_b),
        b1c=col32(b1), b2c=col8(b2),
        w1T=asc(np.asarray(W1, f32).T), w2T=asc(np.asarray(W2, f32).T),
    )
    C, S = _rope_tables()
    in_maps = []
    for core in range(8):
        b = core // 2
        qlo = (core % 2) * NQ
        olo = NQ - qlo
        order = np.concatenate([np.arange(qlo, qlo + NQ), np.arange(olo, olo + NQ)])
        xT = np.asarray(x[b], f32).T
        m = dict(shared)
        m["x_t"] = asc(xT[:, order])
        m["ropeCK"] = asc(C[:, order]).astype(bf)
        m["ropeSK"] = asc(S[:, order]).astype(bf)
        m["ropeCQ"] = (asc(C[:, qlo:qlo + NQ]) / np.float32(8.0)).astype(bf)
        m["ropeSQ"] = (asc(S[:, qlo:qlo + NQ]) / np.float32(8.0)).astype(bf)
        in_maps.append(m)
    return in_maps


def kernel(L0_attn_x, L0_attn_ln_w, L0_attn_ln_b, L0_attn_Wq, L0_attn_bq,
           L0_attn_Wk, L0_attn_bk, L0_attn_Wv, L0_attn_bv, L0_attn_Wo,
           L0_attn_bo, L0_ffn_W1, L0_ffn_b1, L0_ffn_W2, L0_ffn_b2):
    in_maps = make_in_maps(L0_attn_x, L0_attn_ln_w, L0_attn_ln_b, L0_attn_Wq,
                           L0_attn_bq, L0_attn_Wk, L0_attn_bk, L0_attn_Wv,
                           L0_attn_bv, L0_attn_Wo, L0_attn_bo, L0_ffn_W1,
                           L0_ffn_b1, L0_ffn_W2, L0_ffn_b2)
    nc = _get_nc()
    res = run_bass_kernel_spmd(nc, in_maps, core_ids=list(range(8)))
    out = np.empty((4, SEQ, D), dtype=np.float32)
    for core in range(8):
        b = core // 2
        qlo = (core % 2) * NQ
        out[b, qlo:qlo + NQ, :] = np.asarray(res.results[core]["out"], dtype=np.float32).T
    return out
